# version 19
# baseline (speedup 1.0000x reference)
import numpy as np
from contextlib import ExitStack
import concourse.bass as bass
import concourse.mybir as mybir
from concourse.bass_utils import run_bass_kernel_spmd

F32 = mybir.dt.float32
BF16 = mybir.dt.bfloat16
I32 = mybir.dt.int32
AF = mybir.ActivationFunctionType
ALU = mybir.AluOpType

NCORES = 8
NB = 4
T = 2048
BT = NB * T
D = 1024
DI = 2048
DS = 8
HID = 3072
KD = D // 128
KDI = DI // 128
KH = HID // 128
CH = 512
NCH = T // CH
EPS = 1e-5
QS2 = 1.0 / (127.0 * 127.0)

N_SCALES, N_TAPS = 11, 4
OFFSETS = [(1 << j) * tau for j in range(N_SCALES) for tau in range(N_TAPS)]
DOFFS = sorted({o for o in OFFSETS if o < T})
NOFF = len(DOFFS)
MAXOFF = 1536

MODE = "fused"

ENGINES = ("sync", "tensor", "vector", "scalar", "gpsimd")


class Ph:

    def __init__(self, nc, ctx, tag):
        self.nc, self.ctx, self.tag = nc, ctx, tag
        self.q = {k: [] for k in ENGINES}
        self._sems = {}
        self.cnt = {}
        self.entry = []
        self._seen = set()

    def sem(self, name):
        if name not in self._sems:
            self._sems[name] = self.ctx.enter_context(
                self.nc.semaphore(f"{self.tag}_{name}"))
            self.cnt[name] = 0
        return self._sems[name]

    def c(self, name):
        self.sem(name)
        return self.cnt[name]

    def op(self, eng, fn, inc=(), wait=()):
        w = []
        if eng not in self._seen:
            self._seen.add(eng)
            w += [(self._sems[n], v) for n, v in self.entry]
        w += [(self.sem(n), v) for n, v in wait]
        incs = []
        for n, k in inc:
            self.sem(n)
            self.cnt[n] += k
            incs.append((self._sems[n], k))
        self.q[eng].append((w, fn, incs))

    def close(self, *semnames):
        for n in semnames:
            self.sem(n)
        self.entry = [(n, self.cnt[n]) for n in semnames]
        self._seen = set()

    def load(self, dst, src, wait=(), sem="ld"):
        self.op("gpsimd", lambda e, d=dst, s=src: e.dma_start(out=d, in_=s),
                inc=[(sem, 16)], wait=wait)

    def store(self, dst, src, wait=(), sem="st"):
        self.op("gpsimd", lambda e, d=dst, s=src: e.dma_start(out=d, in_=s),
                inc=[(sem, 16)], wait=wait)

    def finalize(self):
        for s in ("st", "stb"):
            if s in self.cnt and self.cnt[s]:
                self.op("gpsimd", None, wait=[(s, self.c(s))])

    def emit(self):
        block = self.ctx.enter_context(self.nc.Block())
        for name in ENGINES:
            items = self.q[name]
            if not items:
                continue

            def body(engine, items=items):
                for w, fn, incs in items:
                    for s, v in w:
                        if v > 0:
                            engine.wait_ge(s, v)
                    inst = fn(engine) if fn is not None else None
                    for s, k in incs:
                        inst = inst.then_inc(s, k)
            getattr(block, name)(body)
        self.q = {k: [] for k in ENGINES}



def host_prep(inp):
    w = {}
    w["w_in"] = np.ascontiguousarray(inp["L0_attn_W_in"])
    dw = inp["L0_attn_dw_kernel"]
    dwd = np.zeros((4, KDI, 128, 128), np.float32)
    for j in range(4):
        for kt in range(KDI):
            np.fill_diagonal(dwd[j, kt], dw[j, kt * 128:(kt + 1) * 128])
    w["dwdiag"] = dwd.reshape(4 * KDI * 128, 128)
    w["pw"] = np.ascontiguousarray(inp["L0_attn_pw_kernel"])
    w["cb2"] = np.ascontiguousarray(inp["L0_attn_conv_bias"].reshape(KDI, 128).T)
    w["wx"] = np.ascontiguousarray(inp["L0_attn_W_x"])
    w["wdt"] = np.ascontiguousarray(inp["L0_attn_W_dt"])
    w["bdt2"] = np.ascontiguousarray(inp["L0_attn_b_dt"].reshape(KDI, 128).T)
    w["alog"] = np.ascontiguousarray(inp["L0_attn_A_log"])
    w["dvec"] = np.ascontiguousarray(inp["L0_attn_D"].reshape(KDI, 128).T)
    w["wout"] = np.ascontiguousarray(inp["L0_attn_W_out"])
    w["wrT"] = np.ascontiguousarray(inp["L0_ffn_W_r"].T)
    w["wkT"] = np.ascontiguousarray(inp["L0_ffn_W_k"].T)
    w["wvT"] = np.ascontiguousarray(inp["L0_ffn_W_v"].T)
    w["mur"] = np.ascontiguousarray(inp["L0_ffn_mu_r"].reshape(KD, 128).T)
    w["muk"] = np.ascontiguousarray(inp["L0_ffn_mu_k"].reshape(KD, 128).T)
    esel = np.zeros((16, 16, 128), np.float32)
    for i in range(16):
        esel[i, i, :] = 1.0
    w["esel"] = esel
    w["wqkv"] = np.ascontiguousarray(inp["L1_attn_w_qkv"])
    w["bqkv2"] = np.ascontiguousarray(inp["L1_attn_b_qkv"].reshape(KH, 128).T)
    w["wgate"] = np.ascontiguousarray(inp["L1_attn_w_gate"])
    w["bgate2"] = np.ascontiguousarray(inp["L1_attn_b_gate"].reshape(KD, 128).T)
    w["wout1"] = np.ascontiguousarray(inp["L1_attn_w_out"])
    w["bout2"] = np.ascontiguousarray(inp["L1_attn_b_out"].reshape(KD, 128).T)
    pb = inp["L1_attn_pos_bias"]
    pb88 = np.zeros((88, 8), np.float32)
    for s in range(44):
        for hp in range(8):
            for hs in range(2):
                pb88[s * 2 + hs, hp] = pb[s, 2 * hp + hs]
    w["pb88"] = pb88
    Emat = np.zeros((NOFF, 128, 88), np.float32)
    for oi, off in enumerate(DOFFS):
        for s in range(44):
            if OFFSETS[s] == off:
                for hs in range(2):
                    Emat[oi, hs * 64:(hs + 1) * 64, s * 2 + hs] = 1.0
    w["Emat"] = Emat.reshape(NOFF * 128, 88)
    G = np.zeros((88, 44), np.float32)
    for s in range(44):
        if OFFSETS[s] < T:
            oi = DOFFS.index(OFFSETS[s])
            for hs in range(2):
                G[s * 2 + hs, oi * 2 + hs] = 1.0
    w["Gmat"] = G
    Smat = np.zeros((NOFF, 44, 128), np.float32)
    for oi in range(NOFF):
        Smat[oi, oi * 2 + 0, 0:64] = 1.0
        Smat[oi, oi * 2 + 1, 64:128] = 1.0
    w["Smat"] = np.ascontiguousarray(Smat.transpose(1, 0, 2).reshape(44, NOFF * 128))
    Eh = np.zeros((88, 2), np.float32)
    for s in range(44):
        for hs in range(2):
            Eh[s * 2 + hs, hs] = 1.0
    w["Ehmat"] = Eh
    Ebc = np.zeros((2, 44), np.float32)
    for oi in range(NOFF):
        Ebc[0, oi * 2 + 0] = 1.0
        Ebc[1, oi * 2 + 1] = 1.0
    w["Ebc"] = Ebc
    w["wgT"] = np.ascontiguousarray(inp["L1_ffn_w_gate"].T)
    w["wuT"] = np.ascontiguousarray(inp["L1_ffn_w_up"].T)
    w["wdT"] = np.ascontiguousarray(inp["L1_ffn_w_down"].T)
    return w


WSPEC = {
    "x": ((D, BT), F32), "w_in": ((D, 2 * DI), F32), "dwdiag": ((4 * KDI * 128, 128), F32),
    "pw": ((DI, DI), F32), "cb2": ((128, KDI), F32), "wx": ((DI, 80), F32),
    "wdt": ((64, DI), F32), "bdt2": ((128, KDI), F32), "alog": ((DI, DS), F32),
    "dvec": ((128, KDI), F32), "wout": ((DI, D), F32),
    "wrT": ((D, D), F32), "wkT": ((D, D), F32), "wvT": ((D, D), F32),
    "mur": ((128, KD), F32), "muk": ((128, KD), F32), "esel": ((16, 16 * 128), F32),
    "wqkv": ((D, 3 * D), F32), "bqkv2": ((128, KH), F32), "wgate": ((D, D), F32),
    "bgate2": ((128, KD), F32), "wout1": ((D, D), F32), "bout2": ((128, KD), F32),
    "pb88": ((88, 8), F32), "Emat": ((NOFF * 128, 88), F32), "Gmat": ((88, 44), F32),
    "Smat": ((44, NOFF * 128), F32), "Ehmat": ((88, 2), F32), "Ebc": ((2, 44), F32),
    "wgT": ((D, HID), I32), "wuT": ((D, HID), I32), "wdT": ((HID, D), I32),
}
SCRATCH = {
    "xn": ((D, BT), BF16), "ffn0": ((D, BT), F32), "sz": ((DI, BT), BF16),
    "xc": ((DI, BT), BF16), "dlr": ((64, BT), BF16), "bc": ((16, BT), BF16),
    "ywz": ((DI, BT), BF16), "x1": ((D, BT), F32), "xn1": ((D, BT), BF16),
    "at1": ((D, BT), BF16), "hmid": ((HID, BT), BF16), "outx": ((D, BT), F32),
}


def wload_bf16(ph, e, src_ap, krows, cols, name, stg=None, stg_e=None, dst=None):
    nc = ph.nc
    if stg is None:
        stg = (stg_e or e)(nc.sbuf_tensor(f"{name}_stg", [128, 2, min(cols, 2048)], F32))
    if dst is None:
        dst = e(nc.sbuf_tensor(f"{name}_bf", [128, krows, cols], BF16))
    cap = stg.shape[2]
    src = src_ap.rearrange("(a p) m -> p a m", p=128)
    i = 0
    for k in range(krows):
        for c0 in range(0, cols, cap):
            cw = min(cap, cols - c0)
            ph.load(stg[:, i % 2, 0:cw], src[:, k, c0:c0 + cw], wait=[("wcv", ph.c("wcv"))])
            ph.op("vector", lambda eng, k=k, c0=c0, cw=cw, i=i: eng.tensor_copy(
                dst[:, k, c0:c0 + cw], stg[:, i % 2, 0:cw]),
                inc=[("wcv", 1)], wait=[("ld", ph.c("ld"))])
            i += 1
    return dst


def wload_bf16_i32(ph, e, src_ap, krows, cols, name, stg_e=None):
    nc = ph.nc
    stg = (stg_e or e)(nc.sbuf_tensor(f"{name}_stg", [128, 2, cols], I32))
    dst = e(nc.sbuf_tensor(f"{name}_bf", [128, krows, cols], BF16))
    src = src_ap.rearrange("(a p) m -> p a m", p=128)
    for k in range(krows):
        ph.load(stg[:, k % 2, :], src[:, k, :], wait=[("wcv", ph.c("wcv"))])
        ph.op("vector", lambda eng, k=k: eng.tensor_copy(dst[:, k, :], stg[:, k % 2, :]),
              inc=[("wcv", 1)], wait=[("ld", ph.c("ld"))])
    return dst


def wload_small_bf16(ph, e, src_ap, rows, cols, name):
    nc = ph.nc
    stg = e(nc.sbuf_tensor(f"{name}_stg", [rows, cols], F32))
    dst = e(nc.sbuf_tensor(f"{name}_bf", [rows, cols], BF16))
    ph.load(stg[:], src_ap)
    ph.op("vector", lambda eng: eng.tensor_copy(dst[:], stg[:]),
          inc=[("wcv", 1)], wait=[("ld", ph.c("ld"))])
    return dst



def build_ln(ph, e, dram, src, dst):
    nc = ph.nc
    xin, xout = dram(src), dram(dst)
    stg = e(nc.sbuf_tensor("ln_stg", [128, 2, T], F32))
    xb = e(nc.sbuf_tensor("ln_xb", [128, KD, T], BF16))
    xq = e(nc.sbuf_tensor("ln_xq", [128, KD, T], BF16))
    m_sb = e(nc.sbuf_tensor("ln_m", [128, T], F32))
    m2 = e(nc.sbuf_tensor("ln_m2", [128, T], F32))
    v_sb = e(nc.sbuf_tensor("ln_v", [128, T], F32))
    rstd = e(nc.sbuf_tensor("ln_rstd", [128, T], F32))
    tmp = e(nc.sbuf_tensor("ln_tmp", [128, T], F32))
    xn_sb = e(nc.sbuf_tensor("ln_xn", [128, KD, T], BF16))
    onesm = e(nc.sbuf_tensor("ln_ones", [128, 128], BF16))
    ps_m = e(nc.psum_tensor("ln_psm", [128, T], F32))
    ps_v = e(nc.psum_tensor("ln_psv", [128, T], F32))

    ph.op("vector", lambda eng: eng.memset(onesm[:], 1.0 / D), inc=[("vec", 1)])
    ph.close("vec")
    for b in range(NB):
        for kt in range(KD):
            idx = b * KD + kt
            ph.load(stg[:, kt % 2, :], xin[kt * 128:(kt + 1) * 128, b * T:(b + 1) * T],
                    wait=[("cv", ph.c("cv"))])
            ph.op("vector", lambda eng, kt=kt: eng.tensor_copy(xb[:, kt, :], stg[:, kt % 2, :]),
                  inc=[("cv", 1)], wait=[("ld", ph.c("ld"))])
            ph.op("scalar", lambda eng, kt=kt: eng.activation(xq[:, kt, :], stg[:, kt % 2, :], AF.Square),
                  inc=[("cv", 1)], wait=[("ld", ph.c("ld"))])
        ph.close("cv")
        for c in range(NCH):
            for kt in range(KD):
                ph.op("tensor", lambda eng, c=c, kt=kt: eng.matmul(
                    ps_m[:, c * CH:(c + 1) * CH], onesm[:], xb[:, kt, c * CH:(c + 1) * CH],
                    start=(kt == 0), stop=(kt == KD - 1)))
            for kt in range(KD):
                ph.op("tensor", lambda eng, c=c, kt=kt: eng.matmul(
                    ps_v[:, c * CH:(c + 1) * CH], onesm[:], xq[:, kt, c * CH:(c + 1) * CH],
                    start=(kt == 0), stop=(kt == KD - 1)),
                    inc=[("mm", 1)] if (c == NCH - 1 and kt == KD - 1) else ())
        ph.close("mm")
        ph.op("scalar", lambda eng: eng.copy(m_sb[:], ps_m[:]), inc=[("sp", 1)])
        ph.close("sp")
        ph.op("vector", lambda eng: eng.tensor_mul(m2[:], m_sb[:], m_sb[:]))
        ph.op("vector", lambda eng: eng.tensor_tensor(v_sb[:], ps_v[:], m2[:], ALU.subtract),
              inc=[("vec", 1)])
        ph.close("vec")
        ph.op("scalar", lambda eng: eng.activation(tmp[:], v_sb[:], AF.Ln, bias=np.float32(EPS)))
        ph.op("scalar", lambda eng: eng.activation(rstd[:], tmp[:], AF.Exp, scale=np.float32(-0.5)),
              inc=[("sp", 1)])
        ph.close("sp")
        for kt in range(KD):
            ph.op("vector", lambda eng, kt=kt: eng.tensor_tensor(tmp[:], xb[:, kt, :], m_sb[:], ALU.subtract))
            ph.op("vector", lambda eng, kt=kt: eng.tensor_mul(xn_sb[:, kt, :], tmp[:], rstd[:]),
                  inc=[("vec", 1)])
            ph.store(xout[kt * 128:(kt + 1) * 128, b * T:(b + 1) * T], xn_sb[:, kt, :],
                     wait=[("vec", ph.c("vec"))])
        ph.close("vec", "st")


def build_l0ffn(ph, e, dram):
    nc = ph.nc
    xn, out = dram("xn"), dram("ffn0")
    wstg = e(nc.sbuf_tensor("ffn_wstg", [128, 2, D], F32))
    wr = wload_bf16(ph, e, dram("wrT"), KD, D, "wr", stg=wstg)
    wk = wload_bf16(ph, e, dram("wkT"), KD, D, "wk", stg=wstg)
    wv = wload_bf16(ph, e, dram("wvT"), KD, D, "wv", stg=wstg)
    mur = e(nc.sbuf_tensor("mur_sb", [128, KD], F32))
    muk = e(nc.sbuf_tensor("muk_sb", [128, KD], F32))
    mur1 = e(nc.sbuf_tensor("mur1_sb", [128, KD], F32))
    muk1 = e(nc.sbuf_tensor("muk1_sb", [128, KD], F32))
    ph.load(mur[:], dram("mur"))
    ph.load(muk[:], dram("muk"))
    ph.op("vector", lambda eng: eng.tensor_scalar(mur1[:], mur[:], -1.0, 1.0, ALU.mult, ALU.add),
          wait=[("ld", ph.c("ld"))])
    ph.op("vector", lambda eng: eng.tensor_scalar(muk1[:], muk[:], -1.0, 1.0, ALU.mult, ALU.add),
          inc=[("vec", 1)])

    xnp = e(nc.sbuf_tensor("ffn_xnp", [128, KD, T + 1], BF16))
    xr = e(nc.sbuf_tensor("ffn_xr", [128, KD, T], BF16))
    xk = xr
    t1 = e(nc.sbuf_tensor("ffn_t1", [128, T], BF16))
    r_sb = e(nc.sbuf_tensor("ffn_r", [128, KD, T], BF16))
    k_sb = e(nc.sbuf_tensor("ffn_k", [128, KD, T], BF16))
    oo = e(nc.sbuf_tensor("ffn_oo", [128, T], F32))
    ps = e(nc.psum_tensor("ffn_ps", [128, 2, T], F32))

    ph.op("vector", lambda eng: eng.memset(xnp[:, :, 0:1], 0.0), inc=[("vec", 1)])
    ph.close("vec", "wcv")

    def mm_block(w, rhs, m, extra_wait):
        for c in range(NCH):
            for k in range(KD):
                ph.op("tensor", lambda eng, m=m, c=c, k=k, w=w, rhs=rhs: eng.matmul(
                    ps[:, m % 2, c * CH:(c + 1) * CH], w[:, k, m * 128:(m + 1) * 128],
                    rhs[:, k, c * CH:(c + 1) * CH], start=(k == 0), stop=(k == KD - 1)),
                    inc=[("mm", 1)] if (c == NCH - 1 and k == KD - 1) else (),
                    wait=extra_wait if (c == 0 and k == 0) else ())

    for b in range(NB):
        ph.load(xnp[:, :, 1:T + 1], xn[:, b * T:(b + 1) * T].rearrange("(k p) t -> p k t", p=128))
        ph.close("ld")
        for kt in range(KD):
            ph.op("vector", lambda eng, kt=kt: eng.tensor_scalar(
                t1[:], xnp[:, kt, 0:T], mur1[:, kt:kt + 1], None, ALU.mult))
            ph.op("vector", lambda eng, kt=kt: eng.scalar_tensor_tensor(
                xr[:, kt, :], xnp[:, kt, 1:T + 1], mur[:, kt:kt + 1], t1[:], ALU.mult, ALU.add),
                inc=[("vec", 1)])
        ph.close("vec")
        for m in range(KD):
            mm_block(wr, xr, m, [("sp", ph.c("sp") - 1)])
            ph.op("scalar", lambda eng, m=m: eng.activation(r_sb[:, m, :], ps[:, m % 2, :], AF.Sigmoid),
                  inc=[("sp", 1)], wait=[("mm", ph.c("mm"))])
        ph.close("mm", "sp")
        for kt in range(KD):
            ph.op("vector", lambda eng, kt=kt: eng.tensor_scalar(
                t1[:], xnp[:, kt, 0:T], muk1[:, kt:kt + 1], None, ALU.mult))
            ph.op("vector", lambda eng, kt=kt: eng.scalar_tensor_tensor(
                xk[:, kt, :], xnp[:, kt, 1:T + 1], muk[:, kt:kt + 1], t1[:], ALU.mult, ALU.add),
                inc=[("vec", 1)])
        ph.close("vec")
        for m in range(KD):
            mm_block(wk, xk, m, [("sp", ph.c("sp") - 1)])
            ph.op("scalar", lambda eng, m=m: eng.activation(k_sb[:, m, :], ps[:, m % 2, :], AF.Relu),
                  inc=[("sp", 1)], wait=[("mm", ph.c("mm"))])
        ph.close("sp")
        for m in range(KD):
            ph.op("vector", lambda eng, m=m: eng.tensor_mul(k_sb[:, m, :], k_sb[:, m, :], k_sb[:, m, :]),
                  inc=[("vec", 1)])
        ph.close("vec")
        for m in range(KD):
            mm_block(wv, k_sb, m, [("vec", ph.c("vec"))])
            ph.op("vector", lambda eng, m=m: eng.tensor_tensor(oo[:], r_sb[:, m, :], ps[:, m % 2, :], ALU.mult),
                  inc=[("vec", 1)], wait=[("mm", ph.c("mm")), ("st", ph.c("st"))])
            ph.store(out[m * 128:(m + 1) * 128, b * T:(b + 1) * T], oo[:],
                     wait=[("vec", ph.c("vec"))])
        ph.close("st", "vec")


def build_mamba_pre(ph, e, dram):
    nc = ph.nc
    xn = dram("xn")
    sz_o, xc_o, dlr_o, bc_o = dram("sz"), dram("xc"), dram("dlr"), dram("bc")
    win = e(nc.sbuf_tensor("win_bf", [128, KD, 2 * DI], BF16))
    pw = e(nc.sbuf_tensor("pw_bf", [128, KDI, DI], BF16))
    dwd = e(nc.sbuf_tensor("dwd_bf", [128, 4 * KDI, 128], BF16))
    wx = e(nc.sbuf_tensor("wx_bf", [128, KDI, 80], BF16))
    cb2 = e(nc.sbuf_tensor("cb2_sb", [128, KDI], F32))
    with nc.sbuf_tensor("mp_wstg", [128, 2, 2 * DI], F32) as wstg:
        wload_bf16(ph, e, dram("w_in"), KD, 2 * DI, "win", stg=wstg, dst=win)
        wload_bf16(ph, e, dram("pw"), KDI, DI, "pw", stg=wstg, dst=pw)
        wload_bf16(ph, e, dram("dwdiag"), 4 * KDI, 128, "dwd", stg=wstg, dst=dwd)
        wload_bf16(ph, e, dram("wx"), KDI, 80, "wx", stg=wstg, dst=wx)
        ph.load(cb2[:], dram("cb2"))
        ph.close("wcv", "ld")

    xnc = e(nc.sbuf_tensor("mp_xnc", [128, KD, CH], BF16))
    xin_c = e(nc.sbuf_tensor("mp_xin", [128, KDI, 3 + CH], BF16))
    cvo = e(nc.sbuf_tensor("mp_cvo", [128, KDI, CH], BF16))
    szc = e(nc.sbuf_tensor("mp_szc", [128, 2, CH], BF16))
    xcc = e(nc.sbuf_tensor("mp_xcc", [128, KDI, CH], BF16))
    dlrc = e(nc.sbuf_tensor("mp_dlr", [64, CH], BF16))
    bbc = e(nc.sbuf_tensor("mp_b", [8, CH], BF16))
    ccc = e(nc.sbuf_tensor("mp_c", [8, CH], BF16))
    ps = e(nc.psum_tensor("mp_ps", [128, 4, CH], F32))
    ps2 = e(nc.psum_tensor("mp_ps2", [128, 4, CH], F32))

    for b in range(NB):
        ph.op("vector", lambda eng: eng.memset(xin_c[:, :, 0:3], 0.0), inc=[("vec", 1)])
        ph.close("vec")
        for c in range(NCH):
            t0 = b * T + c * CH
            gi = b * NCH + c
            if gi == 0:
                ph.load(xnc[:, :, :], xn[:, t0:t0 + CH].rearrange("(k p) t -> p k t", p=128))
            ph.close("ld")
            for m in range(32):
                for k in range(KD):
                    ph.op("tensor", lambda eng, m=m, k=k: eng.matmul(
                        ps[:, m % 4, :], win[:, k, m * 128:(m + 1) * 128], xnc[:, k, :],
                        start=(k == 0), stop=(k == KD - 1)),
                        inc=[("mm", 1)] if k == KD - 1 else (),
                        wait=[("sp", ph.c("sp") - (3 if m > 0 else 0))] if k == 0 else ())
                if m < 16:
                    ph.op("scalar", lambda eng, m=m: eng.copy(xin_c[:, m, 3:3 + CH], ps[:, m % 4, :]),
                          inc=[("sp", 1)],
                          wait=[("mm", ph.c("mm"))] + ([("vec", ph.c("vec"))] if m == 0 else []))
                else:
                    ph.op("scalar", lambda eng, m=m: eng.activation(
                        szc[:, m % 2, :], ps[:, m % 4, :], AF.Silu),
                        inc=[("sp", 1)], wait=[("mm", ph.c("mm")), ("st", ph.c("st"))])
                    if m % 2 == 1:
                        ph.store(sz_o[(m - 17) * 128:(m - 15) * 128, t0:t0 + CH].rearrange(
                            "(k p) t -> p k t", p=128), szc[:, :, :],
                            wait=[("sp", ph.c("sp"))])
            if gi + 1 < NB * NCH:
                t1 = b * T + (c + 1) * CH if c + 1 < NCH else (b + 1) * T
                ph.load(xnc[:, :, :], xn[:, t1:t1 + CH].rearrange("(k p) t -> p k t", p=128),
                        wait=[("mm", ph.c("mm"))])
            ph.close("sp")
            for kt in range(KDI):
                for j in range(4):
                    ph.op("tensor", lambda eng, kt=kt, j=j: eng.matmul(
                        ps2[:, kt % 4, :], dwd[:, j * 16 + kt, :], xin_c[:, kt, j:j + CH],
                        start=(j == 0), stop=(j == 3)),
                        inc=[("mm", 1)] if j == 3 else (),
                        wait=[("sp", ph.c("sp") - (3 if kt > 0 else 0))] if j == 0 else ())
                ph.op("scalar", lambda eng, kt=kt: eng.copy(cvo[:, kt, :], ps2[:, kt % 4, :]),
                      inc=[("sp", 1)], wait=[("mm", ph.c("mm"))])
            ph.close("sp")
            for m in range(KDI):
                for k in range(KDI):
                    ph.op("tensor", lambda eng, m=m, k=k: eng.matmul(
                        ps[:, m % 4, :], pw[:, k, m * 128:(m + 1) * 128], cvo[:, k, :],
                        start=(k == 0), stop=(k == KDI - 1)),
                        inc=[("mm", 1)] if k == KDI - 1 else (),
                        wait=[("sp", ph.c("sp") - 3)] if k == 0 else ())
                ph.op("scalar", lambda eng, m=m: eng.activation(
                    xcc[:, m, :], ps[:, m % 4, :], AF.Silu, bias=cb2[:, m:m + 1], scale=np.float32(1.0)),
                    inc=[("sp", 1)],
                    wait=[("mm", ph.c("mm"))] + ([("st", ph.c("st"))] if m == 0 else []))
            ph.close("sp")
            ph.store(xc_o[:, t0:t0 + CH].rearrange("(k p) t -> p k t", p=128), xcc[:, :, :])
            for k in range(KDI):
                ph.op("tensor", lambda eng, k=k: eng.matmul(
                    ps2[0:64, 0, :], wx[:, k, 0:64], xcc[:, k, :],
                    start=(k == 0), stop=(k == KDI - 1)),
                    wait=[("sp", ph.c("sp"))] if k == 0 else ())
            for k in range(KDI):
                ph.op("tensor", lambda eng, k=k: eng.matmul(
                    ps2[0:8, 1, :], wx[:, k, 64:72], xcc[:, k, :],
                    start=(k == 0), stop=(k == KDI - 1)))
            for k in range(KDI):
                ph.op("tensor", lambda eng, k=k: eng.matmul(
                    ps[0:8, 0, :], wx[:, k, 72:80], xcc[:, k, :],
                    start=(k == 0), stop=(k == KDI - 1)),
                    inc=[("mm", 1)] if k == KDI - 1 else ())
            ph.close("mm")
            ph.op("scalar", lambda eng: eng.copy(dlrc[:], ps2[0:64, 0, :]), wait=[("st", ph.c("st"))])
            ph.op("scalar", lambda eng: eng.copy(bbc[:], ps2[0:8, 1, :]))
            ph.op("scalar", lambda eng: eng.copy(ccc[:], ps[0:8, 0, :]), inc=[("sp", 1)])
            ph.close("sp")
            ph.store(dlr_o[:, t0:t0 + CH], dlrc[:])
            ph.store(bc_o[0:8, t0:t0 + CH], bbc[:])
            ph.store(bc_o[8:16, t0:t0 + CH], ccc[:])
            ph.op("vector", lambda eng: eng.tensor_copy(xin_c[:, :, 0:3], xin_c[:, :, CH:CH + 3]),
                  inc=[("vec", 1)], wait=[("mm", ph.c("mm"))])
            ph.close("vec")


def build_scan(ph, e, dram):
    nc = ph.nc
    xc_i, dlr_i, bc_i, sz_i = dram("xc"), dram("dlr"), dram("bc"), dram("sz")
    ywz_o = dram("ywz")
    wdt = wload_small_bf16(ph, e, dram("wdt"), 64, DI, "wdt")
    eself = e(nc.sbuf_tensor("esel_f", [16, 16, 128], F32))
    eselb = e(nc.sbuf_tensor("esel_b", [16, 16, 128], BF16))
    ph.load(eself[:], dram("esel").rearrange("i (k p) -> k i p", k=16))
    ph.op("vector", lambda eng: eng.tensor_copy(eselb[:], eself[:]),
          inc=[("wcv", 1)], wait=[("ld", ph.c("ld"))])
    bdt2 = e(nc.sbuf_tensor("bdt2_sb", [128, KDI], F32))
    dvec = e(nc.sbuf_tensor("dvec_sb", [128, KDI], F32))
    alog = e(nc.sbuf_tensor("alog_sb", [128, KDI, DS], F32))
    a_mat = e(nc.sbuf_tensor("a_sb", [128, KDI, DS], F32))
    ph.load(bdt2[:], dram("bdt2"))
    ph.load(dvec[:], dram("dvec"))
    ph.load(alog[:], dram("alog").rearrange("(k p) n -> p k n", p=128))
    ph.op("scalar", lambda eng: eng.activation(a_mat[:], alog[:], AF.Exp),
          inc=[("sp", 1)], wait=[("ld", ph.c("ld"))])
    ph.op("vector", lambda eng: eng.tensor_scalar(a_mat[:], a_mat[:], -1.0, None, ALU.mult),
          inc=[("vec", 1)], wait=[("sp", ph.c("sp"))])
    ph.close("vec", "wcv")

    dlr_b = e(nc.sbuf_tensor("sc_dlr", [64, T], BF16))
    bcb = e(nc.sbuf_tensor("sc_bcb", [16, T], BF16))
    bb = e(nc.sbuf_tensor("sc_bb", [128, 16, T], BF16))
    xct = e(nc.sbuf_tensor("sc_xct", [128, 2, T], BF16))
    szt = e(nc.sbuf_tensor("sc_szt", [128, 2, T], BF16))
    delta = e(nc.sbuf_tensor("sc_delta", [128, 2, T], BF16))
    te_b = e(nc.sbuf_tensor("sc_te", [128, T], F32))
    du = e(nc.sbuf_tensor("sc_du", [128, T], BF16))
    a_b = e(nc.sbuf_tensor("sc_a", [128, 4, T], BF16))
    u_b = e(nc.sbuf_tensor("sc_u", [128, T], BF16))
    h_b = u_b
    p_b = e(nc.sbuf_tensor("sc_p", [128, 2, T], BF16))
    y_b = e(nc.sbuf_tensor("sc_y", [128, T], BF16))
    yw = e(nc.sbuf_tensor("sc_yw", [128, 2, T], BF16))
    ps_d = e(nc.psum_tensor("sc_psd", [128, T], F32))
    ps_b = e(nc.psum_tensor("sc_psb", [128, T], F32))

    def sched_chain(dt, prologue):
        for c in range(NCH):
            if c == 0 and not prologue:
                w0 = [("sp", ph.c("sp") - 8)]
            else:
                w0 = ()
            ph.op("tensor", lambda eng, dt=dt, c=c: eng.matmul(
                ps_d[:, c * CH:(c + 1) * CH], wdt[:, dt * 128:(dt + 1) * 128],
                dlr_b[:, c * CH:(c + 1) * CH], start=True, stop=True),
                inc=[("mm", 1)] if c == NCH - 1 else (),
                wait=w0)
        ph.op("scalar", lambda eng, dt=dt: eng.activation(
            te_b[:], ps_d[:], AF.Exp, bias=bdt2[:, dt:dt + 1], scale=np.float32(1.0)),
            wait=[("mm", ph.c("mm")), ("vec", ph.c("vec") - (10 if not prologue else 0))])
        ph.op("scalar", lambda eng, dt=dt: eng.activation(
            delta[:, dt % 2, :], te_b[:], AF.Ln, bias=np.float32(1.0)),
            inc=[("sp", 1)])

    for b in range(NB):
        ph.load(dlr_b[:], dlr_i[:, b * T:(b + 1) * T], wait=[("vec", ph.c("vec"))])
        ph.load(bcb[:], bc_i[:, b * T:(b + 1) * T])
        ph.close("ld")
        for i in range(16):
            pst = ps_b if i % 2 == 0 else ps_d
            for c in range(NCH):
                ph.op("tensor", lambda eng, i=i, c=c, pst=pst: eng.matmul(
                    pst[:, c * CH:(c + 1) * CH], eselb[:, i, :],
                    bcb[:, c * CH:(c + 1) * CH], start=True, stop=True),
                    inc=[("mm", 1)] if c == NCH - 1 else (),
                    wait=([("sp", ph.c("sp"))] if i == 0 else [("sp", ph.c("sp") - 1)]) if c == 0 else ())
            ph.op("scalar", lambda eng, i=i, pst=pst: eng.copy(bb[:, i, :], pst[:]),
                  inc=[("sp", 1)], wait=[("mm", ph.c("mm"))])
        ph.close("sp")
        sched_chain(0, True)
        ph.load(xct[:, 0, :], xc_i[0:128, b * T:(b + 1) * T], wait=[("vec", ph.c("vec"))])
        ph.load(szt[:, 0, :], sz_i[0:128, b * T:(b + 1) * T])
        for dt in range(KDI):
            if dt + 1 < KDI:
                lsem = "ld" if (dt + 1) % 2 == 0 else "ldb"
                ph.load(xct[:, (dt + 1) % 2, :], xc_i[(dt + 1) * 128:(dt + 2) * 128, b * T:(b + 1) * T],
                        wait=[("vec", ph.c("vec") - 2)] if dt >= 1 else (), sem=lsem)
                ph.load(szt[:, (dt + 1) % 2, :], sz_i[(dt + 1) * 128:(dt + 2) * 128, b * T:(b + 1) * T],
                        sem=lsem)
            dsem = "ld" if dt % 2 == 0 else "ldb"
            ph.op("vector", lambda eng, dt=dt: eng.tensor_tensor(du[:], delta[:, dt % 2, :], xct[:, dt % 2, :], ALU.mult),
                  wait=[("sp", ph.c("sp")), (dsem, ph.c(dsem))])
            ph.op("vector", lambda eng, dt=dt: eng.tensor_scalar(
                y_b[:], xct[:, dt % 2, :], dvec[:, dt:dt + 1], None, ALU.mult),
                inc=[("vec", 1)])
            tts_base = ph.c("tts")
            for n in range(DS):
                ph.op("scalar", lambda eng, dt=dt, n=n: eng.activation(
                    a_b[:, n % 4, :], delta[:, dt % 2, :], AF.Exp, scale=a_mat[:, dt, n:n + 1]),
                    inc=[("sp", 1)], wait=[("tts", tts_base + n - 3)] if tts_base + n - 3 > 0 else ())
                ph.op("vector", lambda eng, n=n: eng.tensor_tensor(u_b[:], du[:], bb[:, n, :], ALU.mult))
                ph.op("vector", lambda eng, n=n: eng.tensor_tensor_scan(
                    u_b[:], a_b[:, n % 4, :], u_b[:], 0.0, ALU.mult, ALU.add),
                    inc=[("tts", 1)], wait=[("sp", ph.c("sp"))])
                ph.op("vector", lambda eng, n=n: eng.tensor_tensor(
                    p_b[:, n % 2, :], h_b[:], bb[:, 8 + n, :], ALU.mult))
                ph.op("vector", lambda eng, n=n: eng.tensor_tensor(
                    y_b[:], y_b[:], p_b[:, n % 2, :], ALU.add), inc=[("vec", 1)])
            ssem = "st" if dt % 2 == 0 else "stb"
            ph.op("vector", lambda eng, dt=dt: eng.tensor_tensor(yw[:, dt % 2, :], y_b[:], szt[:, dt % 2, :], ALU.mult),
                  inc=[("vec", 1)], wait=[(ssem, ph.c(ssem))])
            ph.store(ywz_o[dt * 128:(dt + 1) * 128, b * T:(b + 1) * T], yw[:, dt % 2, :],
                     wait=[("vec", ph.c("vec"))], sem=ssem)
            if dt + 1 < KDI:
                sched_chain(dt + 1, False)
        ph.close("vec", "st", "stb", "sp", "tts")


def build_l0out(ph, e, dram):
    nc = ph.nc
    ywz_i, x_i, f_i, x1_o = dram("ywz"), dram("x"), dram("ffn0"), dram("x1")
    wout = wload_bf16(ph, e, dram("wout"), KDI, D, "wo")
    yw_sb = e(nc.sbuf_tensor("lo_yw", [128, KDI, T], BF16))
    xs = e(nc.sbuf_tensor("lo_xs", [128, 2, T], F32))
    fs = e(nc.sbuf_tensor("lo_fs", [128, 2, T], F32))
    t_sb = e(nc.sbuf_tensor("lo_t", [128, T], F32))
    o_sb = e(nc.sbuf_tensor("lo_o", [128, T], F32))
    ps = e(nc.psum_tensor("lo_ps", [128, 2, T], F32))
    ph.close("wcv", "ld")
    for b in range(NB):
        ph.load(yw_sb[:, :, :], ywz_i[:, b * T:(b + 1) * T].rearrange("(k p) t -> p k t", p=128))
        ph.close("ld")
        for m in range(KD):
            ph.load(xs[:, m % 2, :], x_i[m * 128:(m + 1) * 128, b * T:(b + 1) * T],
                    wait=[("vec", ph.c("vec"))])
            ph.load(fs[:, m % 2, :], f_i[m * 128:(m + 1) * 128, b * T:(b + 1) * T])
            for c in range(NCH):
                for k in range(KDI):
                    ph.op("tensor", lambda eng, m=m, c=c, k=k: eng.matmul(
                        ps[:, m % 2, c * CH:(c + 1) * CH], wout[:, k, m * 128:(m + 1) * 128],
                        yw_sb[:, k, c * CH:(c + 1) * CH], start=(k == 0), stop=(k == KDI - 1)),
                        inc=[("mm", 1)] if (c == NCH - 1 and k == KDI - 1) else (),
                        wait=[("vec", ph.c("vec") - 2)] if (c == 0 and k == 0 and ph.c("vec") >= 2) else ())
            ph.op("vector", lambda eng, m=m: eng.tensor_tensor(t_sb[:], ps[:, m % 2, :], xs[:, m % 2, :], ALU.add),
                  wait=[("mm", ph.c("mm")), ("ld", ph.c("ld")), ("st", ph.c("st"))])
            ph.op("vector", lambda eng, m=m: eng.tensor_tensor(o_sb[:], t_sb[:], fs[:, m % 2, :], ALU.add),
                  inc=[("vec", 2)])
            ph.store(x1_o[m * 128:(m + 1) * 128, b * T:(b + 1) * T], o_sb[:],
                     wait=[("vec", ph.c("vec"))])
        ph.close("vec", "st")


def build_l1attn(ph, e, dram):
    nc = ph.nc
    xn1_i, at1_o = dram("xn1"), dram("at1")
    wqkv = e(nc.sbuf_tensor("wqkv_bf", [128, KD, 3 * D], BF16))
    wgate = e(nc.sbuf_tensor("wg1_bf", [128, KD, D], BF16))
    wout1 = e(nc.sbuf_tensor("wo1_bf", [128, KD, D], BF16))
    em = e(nc.sbuf_tensor("em_bf", [128, NOFF, 88], BF16))
    smt = e(nc.sbuf_tensor("smt_b", [44, NOFF, 128], BF16))
    gm = e(nc.sbuf_tensor("gm_bf", [88, 44], BF16))
    ehm = e(nc.sbuf_tensor("ehm_bf", [88, 2], BF16))
    ebc = e(nc.sbuf_tensor("ebc_sb", [2, 44], F32))
    pb = e(nc.sbuf_tensor("pb_sb", [88, 8], F32))
    bqkv = e(nc.sbuf_tensor("bqkv_sb", [128, KH], F32))
    bgate = e(nc.sbuf_tensor("bgate_sb", [128, KD], F32))
    bout = e(nc.sbuf_tensor("bout_sb", [128, KD], F32))
    xn1 = e(nc.sbuf_tensor("la_xn1", [128, KD, T], BF16))
    q_sb = e(nc.sbuf_tensor("la_q", [128, T], BF16))
    kpad = e(nc.sbuf_tensor("la_kp", [128, MAXOFF + T], BF16))
    vpad = e(nc.sbuf_tensor("la_vp", [128, MAXOFF + T], BF16))
    prod = e(nc.sbuf_tensor("la_prod", [128, 2, T], BF16))
    eu = prod[0:88, 0, :]
    buf88 = e(nc.sbuf_tensor("la_b88", [44, T], F32))
    lnt = buf88[0:2, :]
    r2 = lnt
    rb44 = buf88[0:44, :]
    ae = prod[0:44, 1, :]
    abc_sb = e(nc.sbuf_tensor("la_abc", [128, 2, T], BF16))
    p1 = q_sb
    acc = e(nc.sbuf_tensor("la_acc", [128, T], BF16))
    g_sb = prod[:, 0, :]
    ghm = e(nc.sbuf_tensor("la_ghm", [128, KD, T], BF16))
    oo = acc
    ps_a = e(nc.psum_tensor("la_psa", [128, T], F32))
    ps_b = e(nc.psum_tensor("la_psb", [128, T], F32))
    with nc.sbuf_tensor("la_wstg", [128, 2, 448], F32) as wstg:
        wload_bf16(ph, e, dram("wqkv"), KD, 3 * D, "wqkv", stg=wstg, dst=wqkv)
        wload_bf16(ph, e, dram("wgate"), KD, D, "wg1", stg=wstg, dst=wgate)
        wload_bf16(ph, e, dram("wout1"), KD, D, "wo1", stg=wstg, dst=wout1)
        wload_bf16(ph, e, dram("Emat"), NOFF, 88, "em", stg=wstg, dst=em)
        smt2 = smt[:].rearrange("p a q -> p (a q)")
        for c0 in range(0, NOFF * 128, 448):
            cw = min(448, NOFF * 128 - c0)
            ph.load(wstg[0:44, 0, 0:cw], dram("Smat")[:, c0:c0 + cw],
                    wait=[("wcv", ph.c("wcv"))])
            ph.op("vector", lambda eng, c0=c0, cw=cw: eng.tensor_copy(
                smt2[:, c0:c0 + cw], wstg[0:44, 0, 0:cw]),
                inc=[("wcv", 1)], wait=[("ld", ph.c("ld"))])
        ph.load(wstg[0:88, 1, 0:44], dram("Gmat"), wait=[("wcv", ph.c("wcv"))])
        ph.op("vector", lambda eng: eng.tensor_copy(gm[:], wstg[0:88, 1, 0:44]),
              inc=[("wcv", 1)], wait=[("ld", ph.c("ld"))])
        ph.load(wstg[0:88, 0, 0:2], dram("Ehmat"), wait=[("wcv", ph.c("wcv"))])
        ph.op("vector", lambda eng: eng.tensor_copy(ehm[:], wstg[0:88, 0, 0:2]),
              inc=[("wcv", 1)], wait=[("ld", ph.c("ld"))])
        ph.load(ebc[:], dram("Ebc"))
        ph.load(pb[:], dram("pb88"))
        ph.load(bqkv[:], dram("bqkv2"))
        ph.load(bgate[:], dram("bgate2"))
        ph.load(bout[:], dram("bout2"))
        ph.close("wcv", "ld")

    ph.op("vector", lambda eng: eng.memset(kpad[:, 0:MAXOFF], 0.0))
    ph.op("vector", lambda eng: eng.memset(vpad[:, 0:MAXOFF], 0.0), inc=[("vec", 1)])
    ph.close("vec", "wcv", "ld")

    def mm8(wtile, mcol, rhs, dst_note, wait0, pst=None):
        if pst is None:
            pst = ps_a
        for c in range(NCH):
            for k in range(KD):
                ph.op("tensor", lambda eng, c=c, k=k, wtile=wtile, mcol=mcol, rhs=rhs, pst=pst: eng.matmul(
                    pst[:, c * CH:(c + 1) * CH], wtile[:, k, mcol * 128:(mcol + 1) * 128],
                    rhs[:, k, c * CH:(c + 1) * CH], start=(k == 0), stop=(k == KD - 1)),
                    inc=[("mm", 1)] if (c == NCH - 1 and k == KD - 1) else (),
                    wait=wait0 if (c == 0 and k == 0) else ())

    for b in range(NB):
        if b == 0:
            ph.load(xn1[:, :, :], xn1_i[:, 0:T].rearrange("(k p) t -> p k t", p=128))
            ph.close("ld")
        for hp in range(8):
            q_sp = ph.c("sp")
            mm8(wqkv, hp, xn1, "q",
                [("sp", ph.c("sp"))] + ([("ld", ph.c("ld"))] if hp == 0 else []))
            mm8(wqkv, 8 + hp, xn1, "k", [("sp", ph.c("sp"))], pst=ps_b)
            ph.op("scalar", lambda eng, hp=hp: eng.activation(
                q_sb[:], ps_a[:], AF.Identity, bias=bqkv[:, hp:hp + 1], scale=np.float32(1.0)),
                inc=[("sp", 1)], wait=[("mm", ph.c("mm") - 1), ("vec", ph.c("vec"))])
            mm8(wqkv, 16 + hp, xn1, "v", [("sp", ph.c("sp"))])
            ph.op("scalar", lambda eng, hp=hp: eng.activation(
                kpad[:, MAXOFF:], ps_b[:], AF.Identity, bias=bqkv[:, 8 + hp:9 + hp], scale=np.float32(1.0)),
                inc=[("sp", 1)], wait=[("mm", ph.c("mm") - 1)])
            ph.op("scalar", lambda eng, hp=hp: eng.activation(
                vpad[:, MAXOFF:], ps_a[:], AF.Identity, bias=bqkv[:, 16 + hp:17 + hp], scale=np.float32(1.0)),
                inc=[("sp", 1)], wait=[("mm", ph.c("mm"))])
            ph.close("sp")
            mm_base = ph.c("mm")
            for oi, off in enumerate(DOFFS):
                ph.op("vector", lambda eng, oi=oi, off=off: eng.tensor_tensor(
                    prod[:, oi % 2, :], q_sb[:], kpad[:, MAXOFF - off:MAXOFF - off + T], ALU.mult),
                    inc=[("vec", 1)],
                    wait=[("mm", mm_base + oi - 1)] if oi >= 2 else [("mm", mm_base)])
                for c in range(NCH):
                    ph.op("tensor", lambda eng, oi=oi, c=c: eng.matmul(
                        ps_b[0:88, c * CH:(c + 1) * CH], em[:, oi, :],
                        prod[:, oi % 2, c * CH:(c + 1) * CH], start=(oi == 0), stop=(oi == NOFF - 1)),
                        inc=[("mm", 1)] if c == NCH - 1 else (),
                        wait=[("vec", ph.c("vec"))] if c == 0 else ())
            ph.op("scalar", lambda eng, hp=hp: eng.activation(
                eu[:], ps_b[0:88, :], AF.Exp, bias=pb[:, hp:hp + 1], scale=np.float32(0.125)),
                inc=[("sp", 1)], wait=[("mm", ph.c("mm"))])
            ph.close("sp")

            def gate_chunk(c, hp=hp):
                for k in range(KD):
                    ph.op("tensor", lambda eng, c=c, k=k, hp=hp: eng.matmul(
                        ps_b[:, c * CH:(c + 1) * CH], wgate[:, k, hp * 128:(hp + 1) * 128],
                        xn1[:, k, c * CH:(c + 1) * CH], start=(k == 0), stop=(k == KD - 1)),
                        inc=[("mm", 1)] if k == KD - 1 else (),
                        wait=[("sp", ph.c("sp"))] if (c == 0 and k == 0) else ())

            for c in range(NCH):
                ph.op("tensor", lambda eng, c=c: eng.matmul(
                    ps_a[0:2, c * CH:(c + 1) * CH], ehm[:], eu[:, c * CH:(c + 1) * CH],
                    start=True, stop=True),
                    inc=[("mm", 1)] if c == NCH - 1 else ())
            gate_chunk(0)
            ph.op("scalar", lambda eng: eng.activation(lnt[:], ps_a[0:2, :], AF.Ln),
                  wait=[("mm", ph.c("mm") - 1)])
            ph.op("scalar", lambda eng: eng.activation(r2[:], lnt[:], AF.Exp, scale=np.float32(-1.0)),
                  inc=[("sp", 1)])
            gate_chunk(1)
            for c in range(NCH):
                ph.op("tensor", lambda eng, c=c: eng.matmul(
                    ps_a[0:44, c * CH:(c + 1) * CH], ebc[:], r2[:, c * CH:(c + 1) * CH],
                    start=True, stop=True),
                    inc=[("mm", 1)] if c == NCH - 1 else (),
                    wait=[("sp", ph.c("sp"))] if c == 0 else ())
            gate_chunk(2)
            ph.op("scalar", lambda eng: eng.copy(rb44[:], ps_a[0:44, :]),
                  inc=[("sp", 1)], wait=[("mm", ph.c("mm") - 1)])
            for c in range(NCH):
                ph.op("tensor", lambda eng, c=c: eng.matmul(
                    ps_a[64:108, c * CH:(c + 1) * CH], gm[:], eu[:, c * CH:(c + 1) * CH],
                    start=True, stop=True),
                    inc=[("mm", 1)] if c == NCH - 1 else ())
            gate_chunk(3)
            ph.op("vector", lambda eng: eng.tensor_tensor(ae[:], ps_a[64:108, :], rb44[:], ALU.mult),
                  inc=[("vec", 1)], wait=[("mm", ph.c("mm") - 1), ("sp", ph.c("sp"))])
            ph.op("scalar", lambda eng, hp=hp: eng.activation(
                g_sb[:], ps_b[:], AF.Sigmoid, bias=bgate[:, hp:hp + 1], scale=np.float32(1.0)),
                inc=[("gs", 1)], wait=[("mm", ph.c("mm")), ("vec", ph.c("vec"))])
            ph.close("vec")
            vb = ph.c("vec")
            for oi, off in enumerate(DOFFS):
                pp = ps_b if oi % 2 == 0 else ps_a
                for c in range(NCH):
                    ph.op("tensor", lambda eng, oi=oi, c=c, pp=pp: eng.matmul(
                        pp[:, c * CH:(c + 1) * CH], smt[:, oi, :],
                        ae[:, c * CH:(c + 1) * CH], start=True, stop=True),
                        inc=[("mm", 1)] if c == NCH - 1 else (),
                        wait=([("vec", vb), ("gs", ph.c("gs"))] if oi < 2
                              else [("sp", ph.c("sp") - 1)]) if c == 0 else ())
                ph.op("scalar", lambda eng, oi=oi, pp=pp: eng.copy(abc_sb[:, oi % 2, :], pp[:]),
                      inc=[("sp", 1)],
                      wait=[("mm", ph.c("mm"))] + ([("vec", ph.c("vec") - 1)] if oi >= 2 else []))
                ph.op("vector", lambda eng, oi=oi, off=off: eng.tensor_tensor(
                    p1[:], vpad[:, MAXOFF - off:MAXOFF - off + T], abc_sb[:, oi % 2, :], ALU.mult),
                    wait=[("sp", ph.c("sp"))])
                if oi == 0:
                    ph.op("vector", lambda eng: eng.tensor_copy(acc[:], p1[:]), inc=[("vec", 1)])
                else:
                    ph.op("vector", lambda eng: eng.tensor_tensor(acc[:], acc[:], p1[:], ALU.add),
                          inc=[("vec", 1)])
            ph.op("vector", lambda eng, hp=hp: eng.tensor_tensor(ghm[:, hp, :], acc[:], g_sb[:], ALU.mult),
                  inc=[("vec", 1)], wait=[("gs", ph.c("gs"))])
            ph.close("vec", "sp")
        if b + 1 < NB:
            ph.load(xn1[:, :, :], xn1_i[:, (b + 1) * T:(b + 2) * T].rearrange("(k p) t -> p k t", p=128),
                    wait=[("mm", ph.c("mm"))])
        for m in range(KD):
            pst = ps_a if m % 2 == 0 else ps_b
            mm8(wout1, m, ghm, "o",
                [("sp", ph.c("sp") - (1 if m >= 2 else 0)), ("st", ph.c("st"))], pst=pst)
            ph.op("scalar", lambda eng, m=m, pst=pst: eng.activation(
                oo[:], pst[:], AF.Identity, bias=bout[:, m:m + 1], scale=np.float32(1.0)),
                inc=[("sp", 1)], wait=[("mm", ph.c("mm")), ("st", ph.c("st"))])
            ph.store(at1_o[m * 128:(m + 1) * 128, b * T:(b + 1) * T], oo[:],
                     wait=[("sp", ph.c("sp"))])
        ph.close("sp", "st")


def build_l1ffn(ph, e, dram):
    nc = ph.nc
    xn1_i, x1_i, at1_i, h_s, out_o = dram("xn1"), dram("x1"), dram("at1"), dram("hmid"), dram("outx")
    with ExitStack() as we:
        ee = we.enter_context
        wg = wload_bf16_i32(ph, ee, dram("wgT"), KD, HID, "wg")
        wu = wload_bf16_i32(ph, ee, dram("wuT"), KD, HID, "wu")
        xn1 = ee(nc.sbuf_tensor("lf_xn1", [128, KD, T], BF16))
        sg = ee(nc.sbuf_tensor("lf_sg", [128, T], BF16))
        hh = ee(nc.sbuf_tensor("lf_h", [128, 2, T], BF16))
        ps_g = ee(nc.psum_tensor("lf_psg", [128, 2, T // 2], F32))
        ps_u = ee(nc.psum_tensor("lf_psu", [128, 2, T // 2], F32))
        ph.close("wcv", "ld")
        for b in range(NB):
            ph.load(xn1[:, :, :], xn1_i[:, b * T:(b + 1) * T].rearrange("(k p) t -> p k t", p=128))
            ph.close("ld")
            for hm in range(KH):
                for half in range(2):
                    HB = T // 2
                    for c in range(2):
                        for k in range(KD):
                            ph.op("tensor", lambda eng, hm=hm, half=half, c=c, k=k: eng.matmul(
                                ps_g[:, half, c * CH:(c + 1) * CH], wg[:, k, hm * 128:(hm + 1) * 128],
                                xn1[:, k, half * HB + c * CH:half * HB + (c + 1) * CH],
                                start=(k == 0), stop=(k == KD - 1)),
                                wait=[("sp", ph.c("sp") - 1)] if (c == 0 and k == 0) else ())
                    for c in range(2):
                        for k in range(KD):
                            ph.op("tensor", lambda eng, hm=hm, half=half, c=c, k=k: eng.matmul(
                                ps_u[:, half, c * CH:(c + 1) * CH], wu[:, k, hm * 128:(hm + 1) * 128],
                                xn1[:, k, half * HB + c * CH:half * HB + (c + 1) * CH],
                                start=(k == 0), stop=(k == KD - 1)),
                                inc=[("mm", 1)] if (c == 1 and k == KD - 1) else (),
                                wait=[("vec", ph.c("vec") - 1)] if (c == 0 and k == 0) else ())
                    ph.op("scalar", lambda eng, half=half: eng.activation(
                        sg[:, half * HB:(half + 1) * HB], ps_g[:, half, :], AF.Silu,
                        scale=np.float32(1.0 / 127.0)),
                        inc=[("sp", 1)], wait=[("mm", ph.c("mm"))])
                    hsem = "st" if hm % 2 == 0 else "stb"
                    ph.op("vector", lambda eng, half=half, hm=hm: eng.tensor_tensor(
                        hh[:, hm % 2, half * HB:(half + 1) * HB], ps_u[:, half, :], sg[:, half * HB:(half + 1) * HB],
                        ALU.mult),
                        inc=[("vec", 1)], wait=[("sp", ph.c("sp")), (hsem, ph.c(hsem))])
                ph.store(h_s[hm * 128:(hm + 1) * 128, b * T:(b + 1) * T], hh[:, hm % 2, :],
                         wait=[("vec", ph.c("vec"))], sem="st" if hm % 2 == 0 else "stb")
            ph.close("st", "stb", "vec", "sp")
    wd = wload_bf16_i32(ph, e, dram("wdT"), KH, D, "wd")
    hb = e(nc.sbuf_tensor("lf_hb", [128, KH, T], BF16))
    xs = e(nc.sbuf_tensor("lf_xs", [128, 2, T], F32))
    as_ = e(nc.sbuf_tensor("lf_as", [128, 2, T], BF16))
    e_sb = e(nc.sbuf_tensor("lf_e", [128, 2, T], F32))
    ps_d = e(nc.psum_tensor("lf_psd", [128, 2, T], F32))
    ph.close("wcv", "ld", "st", "stb")
    LSEMS = ["ld", "ldb", "ldc", "ldd"]
    for b in range(NB):
        for k in range(4):
            ph.load(hb[:, k, :], h_s[k * 128:(k + 1) * 128, b * T:(b + 1) * T],
                    sem=LSEMS[k % 4])
        ph.load(xs[:, 0, :], x1_i[0:128, b * T:(b + 1) * T],
                wait=[("vec", ph.c("vec"))], sem="lde")
        ph.load(as_[:, 0, :], at1_i[0:128, b * T:(b + 1) * T], sem="lde")
        for m in range(KD):
            if m + 1 < KD:
                lsem = "lde" if (m + 1) % 2 == 0 else "ldf"
                ph.load(xs[:, (m + 1) % 2, :], x1_i[(m + 1) * 128:(m + 2) * 128, b * T:(b + 1) * T],
                        wait=[("vec", ph.c("vec"))], sem=lsem)
                ph.load(as_[:, (m + 1) % 2, :], at1_i[(m + 1) * 128:(m + 2) * 128, b * T:(b + 1) * T],
                        sem=lsem)
            for c in range(NCH):
                for k in range(KH):
                    if m == 0 and c == 0:
                        w0 = [(LSEMS[k % 4], ph.c(LSEMS[k % 4]))]
                        if k == 0:
                            w0.append(("sp", ph.c("sp") - 1))
                    elif c == 0 and k == 0:
                        w0 = [("sp", ph.c("sp") - 1)]
                    else:
                        w0 = ()
                    ph.op("tensor", lambda eng, m=m, c=c, k=k: eng.matmul(
                        ps_d[:, m % 2, c * CH:(c + 1) * CH], wd[:, k, m * 128:(m + 1) * 128],
                        hb[:, k, c * CH:(c + 1) * CH], start=(k == 0), stop=(k == KH - 1)),
                        inc=[("mm", 1)] if (c == NCH - 1 and k == KH - 1) else (),
                        wait=w0)
                    if m == 0 and c == 0 and k + 4 < KH:
                        ph.load(hb[:, k + 4, :], h_s[(k + 4) * 128:(k + 5) * 128, b * T:(b + 1) * T],
                                sem=LSEMS[(k + 4) % 4])
            msem = "st" if m % 2 == 0 else "stb"
            dsem = "lde" if m % 2 == 0 else "ldf"
            ph.op("scalar", lambda eng, m=m: eng.activation(e_sb[:, m % 2, :], ps_d[:, m % 2, :], AF.Identity, scale=np.float32(QS2)),
                  inc=[("sp", 1)], wait=[("mm", ph.c("mm")), (msem, ph.c(msem))])
            ph.op("vector", lambda eng, m=m: eng.tensor_tensor(e_sb[:, m % 2, :], e_sb[:, m % 2, :], xs[:, m % 2, :], ALU.add),
                  wait=[("sp", ph.c("sp")), (dsem, ph.c(dsem))])
            ph.op("vector", lambda eng, m=m: eng.tensor_tensor(e_sb[:, m % 2, :], e_sb[:, m % 2, :], as_[:, m % 2, :], ALU.add),
                  inc=[("vec", 2)])
            ph.store(out_o[m * 128:(m + 1) * 128, b * T:(b + 1) * T], e_sb[:, m % 2, :],
                     wait=[("vec", ph.c("vec"))], sem=msem)
        ph.close("vec", "st", "stb")


PHASES = [
    ("ln0", lambda ph, e, dram: build_ln(ph, e, dram, "x", "xn"),
     ["x"], ["xn"]),
    ("l0ffn", build_l0ffn, ["xn", "wrT", "wkT", "wvT", "mur", "muk"], ["ffn0"]),
    ("mambapre", build_mamba_pre,
     ["xn", "w_in", "dwdiag", "pw", "cb2", "wx"], ["sz", "xc", "dlr", "bc"]),
    ("scan", build_scan,
     ["xc", "dlr", "bc", "sz", "wdt", "bdt2", "alog", "dvec", "esel"], ["ywz"]),
    ("l0out", build_l0out, ["ywz", "x", "ffn0", "wout"], ["x1"]),
    ("ln1", lambda ph, e, dram: build_ln(ph, e, dram, "x1", "xn1"),
     ["x1"], ["xn1"]),
    ("l1attn", build_l1attn,
     ["xn1", "wqkv", "bqkv2", "wgate", "bgate2", "wout1", "bout2", "pb88",
      "Emat", "Gmat", "Smat", "Ehmat", "Ebc"], ["at1"]),
    ("l1ffn", build_l1ffn, ["xn1", "x1", "at1", "wgT", "wuT", "wdT"], ["outx"]),
]


def build_phase_prog(name, builder, ins, outs):
    nc = bass.Bass("TRN2", target_bir_lowering=False)
    ctx = ExitStack()
    aps = {}
    for n in ins:
        shape, dt = WSPEC[n] if n in WSPEC else SCRATCH[n]
        aps[n] = nc.dram_tensor(n, list(shape), dt, kind="ExternalInput").ap()
    for n in outs:
        shape, dt = SCRATCH[n]
        aps[n] = nc.dram_tensor(n, list(shape), dt, kind="ExternalOutput").ap()
    if name == "l1ffn":
        shape, dt = SCRATCH["hmid"]
        aps["hmid"] = nc.dram_tensor("hmid", list(shape), dt).ap()
    with ctx:
        ph = Ph(nc, ctx, name)
        builder(ph, ctx.enter_context, lambda n: aps[n])
        ph.finalize()
        ph.emit()
    return nc


class NCProxy:

    def __init__(self, nc, prefix):
        self._nc = nc
        self._prefix = prefix

    def __getattr__(self, a):
        return getattr(self._nc, a)

    def sbuf_tensor(self, name, *a, **k):
        return self._nc.sbuf_tensor(self._prefix + name, *a, **k)

    def psum_tensor(self, name, *a, **k):
        return self._nc.psum_tensor(self._prefix + name, *a, **k)


def build_fused():
    nc = bass.Bass("TRN2", target_bir_lowering=False)
    aps = {}
    for n, (shape, dt) in WSPEC.items():
        aps[n] = nc.dram_tensor(n, list(shape), dt, kind="ExternalInput").ap()
    for n, (shape, dt) in SCRATCH.items():
        if n == "outx":
            aps[n] = nc.dram_tensor(n, list(shape), dt, kind="ExternalOutput").ap()
        else:
            aps[n] = nc.dram_tensor(n, list(shape), dt).ap()
    ctx = ExitStack()
    with ctx:
        ph = Ph(nc, ctx, "mk")
        for pi, (name, builder, ins, outs) in enumerate(PHASES):
            ph.nc = NCProxy(nc, f"p{pi}_")
            with ExitStack() as pctx:
                builder(ph, pctx.enter_context, lambda n: aps[n])
                ph.close(*list(ph.cnt.keys()))
        ph.nc = nc
        ph.finalize()
        ph.emit()
    return nc


_PROG_CACHE = {}


def kernel(**inputs):
    w = host_prep(inputs)
    x = inputs["L0_attn_x"]
    state = []
    for c in range(NCORES):
        xs = np.ascontiguousarray(
            x[c * NB:(c + 1) * NB].transpose(2, 0, 1).reshape(D, BT))
        state.append({"x": xs, **w})

    if MODE == "fused":
        if "fused" not in _PROG_CACHE:
            _PROG_CACHE["fused"] = build_fused()
        nc = _PROG_CACHE["fused"]
        in_maps = [dict(state[c]) for c in range(NCORES)]
        res = run_bass_kernel_spmd(nc, in_maps, core_ids=list(range(NCORES)))
        for c in range(NCORES):
            state[c]["outx"] = np.asarray(res.results[c]["outx"])
    else:
        for name, builder, ins, outs in PHASES:
            key = name
            if key not in _PROG_CACHE:
                _PROG_CACHE[key] = build_phase_prog(name, builder, ins, outs)
            nc = _PROG_CACHE[key]
            in_maps = [{n: state[c][n] for n in ins} for c in range(NCORES)]
            res = run_bass_kernel_spmd(nc, in_maps, core_ids=list(range(NCORES)))
            for c in range(NCORES):
                for n in outs:
                    state[c][n] = np.asarray(res.results[c][n])

    out = np.empty((32, T, D), np.float32)
    for c in range(NCORES):
        o = state[c]["outx"].reshape(D, NB, T).transpose(1, 2, 0)
        out[c * NB:(c + 1) * NB] = o
    return out


# version 20
# speedup vs baseline: 1.1521x; 1.1521x over previous
import numpy as np
from contextlib import ExitStack
import concourse.bass as bass
import concourse.mybir as mybir
from concourse.bass_utils import run_bass_kernel_spmd

F32 = mybir.dt.float32
BF16 = mybir.dt.bfloat16
I32 = mybir.dt.int32
AF = mybir.ActivationFunctionType
ALU = mybir.AluOpType

NCORES = 8
NB = 4
T = 2048
BT = NB * T
D = 1024
DI = 2048
DS = 8
HID = 3072
KD = D // 128
KDI = DI // 128
KH = HID // 128
CH = 512
NCH = T // CH
EPS = 1e-5
QS2 = 1.0 / (127.0 * 127.0)

N_SCALES, N_TAPS = 11, 4
OFFSETS = [(1 << j) * tau for j in range(N_SCALES) for tau in range(N_TAPS)]
DOFFS = sorted({o for o in OFFSETS if o < T})
NOFF = len(DOFFS)
MAXOFF = 1536

MODE = "fused"

ENGINES = ("sync", "tensor", "vector", "scalar", "gpsimd")


class Ph:

    def __init__(self, nc, ctx, tag):
        self.nc, self.ctx, self.tag = nc, ctx, tag
        self.q = {k: [] for k in ENGINES}
        self._sems = {}
        self.cnt = {}
        self.entry = []
        self._seen = set()

    def sem(self, name):
        if name not in self._sems:
            self._sems[name] = self.ctx.enter_context(
                self.nc.semaphore(f"{self.tag}_{name}"))
            self.cnt[name] = 0
        return self._sems[name]

    def c(self, name):
        self.sem(name)
        return self.cnt[name]

    def op(self, eng, fn, inc=(), wait=()):
        w = []
        if eng not in self._seen:
            self._seen.add(eng)
            w += [(self._sems[n], v) for n, v in self.entry]
        w += [(self.sem(n), v) for n, v in wait]
        incs = []
        for n, k in inc:
            self.sem(n)
            self.cnt[n] += k
            incs.append((self._sems[n], k))
        self.q[eng].append((w, fn, incs))

    def close(self, *semnames):
        for n in semnames:
            self.sem(n)
        self.entry = [(n, self.cnt[n]) for n in semnames]
        self._seen = set()

    def load(self, dst, src, wait=(), sem="ld"):
        self.op("gpsimd", lambda e, d=dst, s=src: e.dma_start(out=d, in_=s),
                inc=[(sem, 16)], wait=wait)

    def store(self, dst, src, wait=(), sem="st"):
        self.op("gpsimd", lambda e, d=dst, s=src: e.dma_start(out=d, in_=s),
                inc=[(sem, 16)], wait=wait)

    def finalize(self):
        for s in ("st", "stb"):
            if s in self.cnt and self.cnt[s]:
                self.op("gpsimd", None, wait=[(s, self.c(s))])

    def emit(self):
        block = self.ctx.enter_context(self.nc.Block())
        for name in ENGINES:
            items = self.q[name]
            if not items:
                continue

            def body(engine, items=items):
                for w, fn, incs in items:
                    for s, v in w:
                        if v > 0:
                            engine.wait_ge(s, v)
                    inst = fn(engine) if fn is not None else None
                    for s, k in incs:
                        inst = inst.then_inc(s, k)
            getattr(block, name)(body)
        self.q = {k: [] for k in ENGINES}



def host_prep(inp):
    w = {}
    w["w_in"] = np.ascontiguousarray(inp["L0_attn_W_in"])
    dw = inp["L0_attn_dw_kernel"]
    dwd = np.zeros((4, KDI, 128, 128), np.float32)
    for j in range(4):
        for kt in range(KDI):
            np.fill_diagonal(dwd[j, kt], dw[j, kt * 128:(kt + 1) * 128])
    w["dwdiag"] = dwd.reshape(4 * KDI * 128, 128)
    w["pw"] = np.ascontiguousarray(inp["L0_attn_pw_kernel"])
    w["cb2"] = np.ascontiguousarray(inp["L0_attn_conv_bias"].reshape(KDI, 128).T)
    w["wx"] = np.ascontiguousarray(inp["L0_attn_W_x"])
    w["wdt"] = np.ascontiguousarray(inp["L0_attn_W_dt"])
    w["bdt2"] = np.ascontiguousarray(inp["L0_attn_b_dt"].reshape(KDI, 128).T)
    w["alog"] = np.ascontiguousarray(inp["L0_attn_A_log"])
    w["dvec"] = np.ascontiguousarray(inp["L0_attn_D"].reshape(KDI, 128).T)
    w["wout"] = np.ascontiguousarray(inp["L0_attn_W_out"])
    w["wrT"] = np.ascontiguousarray(inp["L0_ffn_W_r"].T)
    w["wkT"] = np.ascontiguousarray(inp["L0_ffn_W_k"].T)
    w["wvT"] = np.ascontiguousarray(inp["L0_ffn_W_v"].T)
    w["mur"] = np.ascontiguousarray(inp["L0_ffn_mu_r"].reshape(KD, 128).T)
    w["muk"] = np.ascontiguousarray(inp["L0_ffn_mu_k"].reshape(KD, 128).T)
    esel = np.zeros((16, 16, 128), np.float32)
    for i in range(16):
        esel[i, i, :] = 1.0
    w["esel"] = esel
    w["wqkv"] = np.ascontiguousarray(inp["L1_attn_w_qkv"])
    w["bqkv2"] = np.ascontiguousarray(inp["L1_attn_b_qkv"].reshape(KH, 128).T)
    w["wgate"] = np.ascontiguousarray(inp["L1_attn_w_gate"])
    w["bgate2"] = np.ascontiguousarray(inp["L1_attn_b_gate"].reshape(KD, 128).T)
    w["wout1"] = np.ascontiguousarray(inp["L1_attn_w_out"])
    w["bout2"] = np.ascontiguousarray(inp["L1_attn_b_out"].reshape(KD, 128).T)
    pb = inp["L1_attn_pos_bias"]
    pb88 = np.zeros((88, 8), np.float32)
    for s in range(44):
        for hp in range(8):
            for hs in range(2):
                pb88[s * 2 + hs, hp] = pb[s, 2 * hp + hs]
    w["pb88"] = pb88
    Emat = np.zeros((NOFF, 128, 88), np.float32)
    for oi, off in enumerate(DOFFS):
        for s in range(44):
            if OFFSETS[s] == off:
                for hs in range(2):
                    Emat[oi, hs * 64:(hs + 1) * 64, s * 2 + hs] = 1.0
    w["Emat"] = Emat.reshape(NOFF * 128, 88)
    G = np.zeros((88, 44), np.float32)
    for s in range(44):
        if OFFSETS[s] < T:
            oi = DOFFS.index(OFFSETS[s])
            for hs in range(2):
                G[s * 2 + hs, oi * 2 + hs] = 1.0
    w["Gmat"] = G
    Smat = np.zeros((NOFF, 44, 128), np.float32)
    for oi in range(NOFF):
        Smat[oi, oi * 2 + 0, 0:64] = 1.0
        Smat[oi, oi * 2 + 1, 64:128] = 1.0
    w["Smat"] = np.ascontiguousarray(Smat.transpose(1, 0, 2).reshape(44, NOFF * 128))
    Eh = np.zeros((88, 2), np.float32)
    for s in range(44):
        for hs in range(2):
            Eh[s * 2 + hs, hs] = 1.0
    w["Ehmat"] = Eh
    Ebc = np.zeros((2, 44), np.float32)
    for oi in range(NOFF):
        Ebc[0, oi * 2 + 0] = 1.0
        Ebc[1, oi * 2 + 1] = 1.0
    w["Ebc"] = Ebc
    w["wgT"] = np.ascontiguousarray(inp["L1_ffn_w_gate"].T)
    w["wuT"] = np.ascontiguousarray(inp["L1_ffn_w_up"].T)
    w["wdT"] = np.ascontiguousarray(inp["L1_ffn_w_down"].T)
    return w


WSPEC = {
    "x": ((D, BT), F32), "w_in": ((D, 2 * DI), F32), "dwdiag": ((4 * KDI * 128, 128), F32),
    "pw": ((DI, DI), F32), "cb2": ((128, KDI), F32), "wx": ((DI, 80), F32),
    "wdt": ((64, DI), F32), "bdt2": ((128, KDI), F32), "alog": ((DI, DS), F32),
    "dvec": ((128, KDI), F32), "wout": ((DI, D), F32),
    "wrT": ((D, D), F32), "wkT": ((D, D), F32), "wvT": ((D, D), F32),
    "mur": ((128, KD), F32), "muk": ((128, KD), F32), "esel": ((16, 16 * 128), F32),
    "wqkv": ((D, 3 * D), F32), "bqkv2": ((128, KH), F32), "wgate": ((D, D), F32),
    "bgate2": ((128, KD), F32), "wout1": ((D, D), F32), "bout2": ((128, KD), F32),
    "pb88": ((88, 8), F32), "Emat": ((NOFF * 128, 88), F32), "Gmat": ((88, 44), F32),
    "Smat": ((44, NOFF * 128), F32), "Ehmat": ((88, 2), F32), "Ebc": ((2, 44), F32),
    "wgT": ((D, HID), I32), "wuT": ((D, HID), I32), "wdT": ((HID, D), I32),
}
SCRATCH = {
    "xn": ((D, BT), BF16), "ffn0": ((D, BT), F32), "sz": ((DI, BT), BF16),
    "xc": ((DI, BT), BF16), "dlr": ((64, BT), BF16), "bc": ((16, BT), BF16),
    "ywz": ((DI, BT), BF16), "x1": ((D, BT), F32), "xn1": ((D, BT), BF16),
    "at1": ((D, BT), BF16), "hmid": ((HID, BT), BF16), "outx": ((D, BT), F32),
}


def wload_bf16(ph, e, src_ap, krows, cols, name, stg=None, stg_e=None, dst=None):
    nc = ph.nc
    if stg is None:
        stg = (stg_e or e)(nc.sbuf_tensor(f"{name}_stg", [128, 2, min(cols, 2048)], F32))
    if dst is None:
        dst = e(nc.sbuf_tensor(f"{name}_bf", [128, krows, cols], BF16))
    cap = stg.shape[2]
    src = src_ap.rearrange("(a p) m -> p a m", p=128)
    i = 0
    for k in range(krows):
        for c0 in range(0, cols, cap):
            cw = min(cap, cols - c0)
            ph.load(stg[:, i % 2, 0:cw], src[:, k, c0:c0 + cw], wait=[("wcv", ph.c("wcv"))])
            ph.op("vector", lambda eng, k=k, c0=c0, cw=cw, i=i: eng.tensor_copy(
                dst[:, k, c0:c0 + cw], stg[:, i % 2, 0:cw]),
                inc=[("wcv", 1)], wait=[("ld", ph.c("ld"))])
            i += 1
    return dst


def wload_bf16_i32(ph, e, src_ap, krows, cols, name, stg_e=None):
    nc = ph.nc
    stg = (stg_e or e)(nc.sbuf_tensor(f"{name}_stg", [128, 2, cols], I32))
    dst = e(nc.sbuf_tensor(f"{name}_bf", [128, krows, cols], BF16))
    src = src_ap.rearrange("(a p) m -> p a m", p=128)
    for k in range(krows):
        ph.load(stg[:, k % 2, :], src[:, k, :], wait=[("wcv", ph.c("wcv"))])
        ph.op("vector", lambda eng, k=k: eng.tensor_copy(dst[:, k, :], stg[:, k % 2, :]),
              inc=[("wcv", 1)], wait=[("ld", ph.c("ld"))])
    return dst


def wload_small_bf16(ph, e, src_ap, rows, cols, name):
    nc = ph.nc
    stg = e(nc.sbuf_tensor(f"{name}_stg", [rows, cols], F32))
    dst = e(nc.sbuf_tensor(f"{name}_bf", [rows, cols], BF16))
    ph.load(stg[:], src_ap)
    ph.op("vector", lambda eng: eng.tensor_copy(dst[:], stg[:]),
          inc=[("wcv", 1)], wait=[("ld", ph.c("ld"))])
    return dst



def build_ln(ph, e, dram, src, dst):
    nc = ph.nc
    xin, xout = dram(src), dram(dst)
    stg = e(nc.sbuf_tensor("ln_stg", [128, 2, T], F32))
    xb = e(nc.sbuf_tensor("ln_xb", [128, KD, T], BF16))
    xq = e(nc.sbuf_tensor("ln_xq", [128, KD, T], BF16))
    m_sb = e(nc.sbuf_tensor("ln_m", [128, T], F32))
    m2 = e(nc.sbuf_tensor("ln_m2", [128, T], F32))
    v_sb = e(nc.sbuf_tensor("ln_v", [128, T], F32))
    rstd = e(nc.sbuf_tensor("ln_rstd", [128, T], F32))
    tmp = e(nc.sbuf_tensor("ln_tmp", [128, T], F32))
    xn_sb = e(nc.sbuf_tensor("ln_xn", [128, KD, T], BF16))
    onesm = e(nc.sbuf_tensor("ln_ones", [128, 128], BF16))
    ps_m = e(nc.psum_tensor("ln_psm", [128, T], F32))
    ps_v = e(nc.psum_tensor("ln_psv", [128, T], F32))

    ph.op("vector", lambda eng: eng.memset(onesm[:], 1.0 / D), inc=[("vec", 1)])
    ph.close("vec")
    for b in range(NB):
        for kt in range(KD):
            idx = b * KD + kt
            ph.load(stg[:, kt % 2, :], xin[kt * 128:(kt + 1) * 128, b * T:(b + 1) * T],
                    wait=[("cv", ph.c("cv"))])
            ph.op("vector", lambda eng, kt=kt: eng.tensor_copy(xb[:, kt, :], stg[:, kt % 2, :]),
                  inc=[("cv", 1)], wait=[("ld", ph.c("ld"))])
            ph.op("scalar", lambda eng, kt=kt: eng.activation(xq[:, kt, :], stg[:, kt % 2, :], AF.Square),
                  inc=[("cv", 1)], wait=[("ld", ph.c("ld"))])
        ph.close("cv")
        for c in range(NCH):
            for kt in range(KD):
                ph.op("tensor", lambda eng, c=c, kt=kt: eng.matmul(
                    ps_m[:, c * CH:(c + 1) * CH], onesm[:], xb[:, kt, c * CH:(c + 1) * CH],
                    start=(kt == 0), stop=(kt == KD - 1)))
            for kt in range(KD):
                ph.op("tensor", lambda eng, c=c, kt=kt: eng.matmul(
                    ps_v[:, c * CH:(c + 1) * CH], onesm[:], xq[:, kt, c * CH:(c + 1) * CH],
                    start=(kt == 0), stop=(kt == KD - 1)),
                    inc=[("mm", 1)] if (c == NCH - 1 and kt == KD - 1) else ())
        ph.close("mm")
        ph.op("scalar", lambda eng: eng.copy(m_sb[:], ps_m[:]), inc=[("sp", 1)])
        ph.close("sp")
        ph.op("vector", lambda eng: eng.tensor_mul(m2[:], m_sb[:], m_sb[:]))
        ph.op("vector", lambda eng: eng.tensor_tensor(v_sb[:], ps_v[:], m2[:], ALU.subtract),
              inc=[("vec", 1)])
        ph.close("vec")
        ph.op("scalar", lambda eng: eng.activation(tmp[:], v_sb[:], AF.Ln, bias=np.float32(EPS)))
        ph.op("scalar", lambda eng: eng.activation(rstd[:], tmp[:], AF.Exp, scale=np.float32(-0.5)),
              inc=[("sp", 1)])
        ph.close("sp")
        for kt in range(KD):
            ph.op("vector", lambda eng, kt=kt: eng.tensor_tensor(tmp[:], xb[:, kt, :], m_sb[:], ALU.subtract))
            ph.op("vector", lambda eng, kt=kt: eng.tensor_mul(xn_sb[:, kt, :], tmp[:], rstd[:]),
                  inc=[("vec", 1)])
            ph.store(xout[kt * 128:(kt + 1) * 128, b * T:(b + 1) * T], xn_sb[:, kt, :],
                     wait=[("vec", ph.c("vec"))])
        ph.close("vec", "st")


def build_l0ffn(ph, e, dram):
    nc = ph.nc
    xn, out = dram("xn"), dram("ffn0")
    wstg = e(nc.sbuf_tensor("ffn_wstg", [128, 2, D], F32))
    wr = wload_bf16(ph, e, dram("wrT"), KD, D, "wr", stg=wstg)
    wk = wload_bf16(ph, e, dram("wkT"), KD, D, "wk", stg=wstg)
    wv = wload_bf16(ph, e, dram("wvT"), KD, D, "wv", stg=wstg)
    mur = e(nc.sbuf_tensor("mur_sb", [128, KD], F32))
    muk = e(nc.sbuf_tensor("muk_sb", [128, KD], F32))
    mur1 = e(nc.sbuf_tensor("mur1_sb", [128, KD], F32))
    muk1 = e(nc.sbuf_tensor("muk1_sb", [128, KD], F32))
    ph.load(mur[:], dram("mur"))
    ph.load(muk[:], dram("muk"))
    ph.op("vector", lambda eng: eng.tensor_scalar(mur1[:], mur[:], -1.0, 1.0, ALU.mult, ALU.add),
          wait=[("ld", ph.c("ld"))])
    ph.op("vector", lambda eng: eng.tensor_scalar(muk1[:], muk[:], -1.0, 1.0, ALU.mult, ALU.add),
          inc=[("vec", 1)])

    xnp = e(nc.sbuf_tensor("ffn_xnp", [128, KD, T + 1], BF16))
    xr = e(nc.sbuf_tensor("ffn_xr", [128, KD, T], BF16))
    xk = xr
    t1 = e(nc.sbuf_tensor("ffn_t1", [128, T], BF16))
    r_sb = e(nc.sbuf_tensor("ffn_r", [128, KD, T], BF16))
    k_sb = e(nc.sbuf_tensor("ffn_k", [128, KD, T], BF16))
    oo = e(nc.sbuf_tensor("ffn_oo", [128, T], F32))
    ps = e(nc.psum_tensor("ffn_ps", [128, 2, T], F32))

    ph.op("vector", lambda eng: eng.memset(xnp[:, :, 0:1], 0.0), inc=[("vec", 1)])
    ph.close("vec", "wcv")

    def mm_block(w, rhs, m, extra_wait):
        for c in range(NCH):
            for k in range(KD):
                ph.op("tensor", lambda eng, m=m, c=c, k=k, w=w, rhs=rhs: eng.matmul(
                    ps[:, m % 2, c * CH:(c + 1) * CH], w[:, k, m * 128:(m + 1) * 128],
                    rhs[:, k, c * CH:(c + 1) * CH], start=(k == 0), stop=(k == KD - 1)),
                    inc=[("mm", 1)] if (c == NCH - 1 and k == KD - 1) else (),
                    wait=extra_wait if (c == 0 and k == 0) else ())

    for b in range(NB):
        ph.load(xnp[:, :, 1:T + 1], xn[:, b * T:(b + 1) * T].rearrange("(k p) t -> p k t", p=128))
        ph.close("ld")
        for kt in range(KD):
            ph.op("vector", lambda eng, kt=kt: eng.tensor_scalar(
                t1[:], xnp[:, kt, 0:T], mur1[:, kt:kt + 1], None, ALU.mult))
            ph.op("vector", lambda eng, kt=kt: eng.scalar_tensor_tensor(
                xr[:, kt, :], xnp[:, kt, 1:T + 1], mur[:, kt:kt + 1], t1[:], ALU.mult, ALU.add),
                inc=[("vec", 1)])
        ph.close("vec")
        for m in range(KD):
            mm_block(wr, xr, m, [("sp", ph.c("sp") - 1)])
            ph.op("scalar", lambda eng, m=m: eng.activation(r_sb[:, m, :], ps[:, m % 2, :], AF.Sigmoid),
                  inc=[("sp", 1)], wait=[("mm", ph.c("mm"))])
        ph.close("mm", "sp")
        for kt in range(KD):
            ph.op("vector", lambda eng, kt=kt: eng.tensor_scalar(
                t1[:], xnp[:, kt, 0:T], muk1[:, kt:kt + 1], None, ALU.mult))
            ph.op("vector", lambda eng, kt=kt: eng.scalar_tensor_tensor(
                xk[:, kt, :], xnp[:, kt, 1:T + 1], muk[:, kt:kt + 1], t1[:], ALU.mult, ALU.add),
                inc=[("vec", 1)])
        ph.close("vec")
        for m in range(KD):
            mm_block(wk, xk, m, [("sp", ph.c("sp") - 1)])
            ph.op("scalar", lambda eng, m=m: eng.activation(k_sb[:, m, :], ps[:, m % 2, :], AF.Relu),
                  inc=[("sp", 1)], wait=[("mm", ph.c("mm"))])
        ph.close("sp")
        for m in range(KD):
            ph.op("vector", lambda eng, m=m: eng.tensor_mul(k_sb[:, m, :], k_sb[:, m, :], k_sb[:, m, :]),
                  inc=[("vec", 1)])
        ph.close("vec")
        for m in range(KD):
            mm_block(wv, k_sb, m, [("vec", ph.c("vec"))])
            ph.op("vector", lambda eng, m=m: eng.tensor_tensor(oo[:], r_sb[:, m, :], ps[:, m % 2, :], ALU.mult),
                  inc=[("vec", 1)], wait=[("mm", ph.c("mm")), ("st", ph.c("st"))])
            ph.store(out[m * 128:(m + 1) * 128, b * T:(b + 1) * T], oo[:],
                     wait=[("vec", ph.c("vec"))])
        ph.close("st", "vec")


def build_mamba_pre(ph, e, dram):
    nc = ph.nc
    xn = dram("xn")
    sz_o, xc_o, dlr_o, bc_o = dram("sz"), dram("xc"), dram("dlr"), dram("bc")
    win = e(nc.sbuf_tensor("win_bf", [128, KD, 2 * DI], BF16))
    pw = e(nc.sbuf_tensor("pw_bf", [128, KDI, DI], BF16))
    dwd = e(nc.sbuf_tensor("dwd_bf", [128, 4 * KDI, 128], BF16))
    wx = e(nc.sbuf_tensor("wx_bf", [128, KDI, 80], BF16))
    cb2 = e(nc.sbuf_tensor("cb2_sb", [128, KDI], F32))
    with nc.sbuf_tensor("mp_wstg", [128, 2, 2 * DI], F32) as wstg:
        wload_bf16(ph, e, dram("w_in"), KD, 2 * DI, "win", stg=wstg, dst=win)
        wload_bf16(ph, e, dram("pw"), KDI, DI, "pw", stg=wstg, dst=pw)
        wload_bf16(ph, e, dram("dwdiag"), 4 * KDI, 128, "dwd", stg=wstg, dst=dwd)
        wload_bf16(ph, e, dram("wx"), KDI, 80, "wx", stg=wstg, dst=wx)
        ph.load(cb2[:], dram("cb2"))
        ph.close("wcv", "ld")

    xnc = e(nc.sbuf_tensor("mp_xnc", [128, KD, CH], BF16))
    xin_c = e(nc.sbuf_tensor("mp_xin", [128, KDI, 3 + CH], BF16))
    cvo = e(nc.sbuf_tensor("mp_cvo", [128, KDI, CH], BF16))
    szc = e(nc.sbuf_tensor("mp_szc", [128, 2, CH], BF16))
    xcc = e(nc.sbuf_tensor("mp_xcc", [128, KDI, CH], BF16))
    dlrc = e(nc.sbuf_tensor("mp_dlr", [64, CH], BF16))
    bbc = e(nc.sbuf_tensor("mp_b", [8, CH], BF16))
    ccc = e(nc.sbuf_tensor("mp_c", [8, CH], BF16))
    ps = e(nc.psum_tensor("mp_ps", [128, 4, CH], F32))
    ps2 = e(nc.psum_tensor("mp_ps2", [128, 4, CH], F32))

    for b in range(NB):
        ph.op("vector", lambda eng: eng.memset(xin_c[:, :, 0:3], 0.0), inc=[("vec", 1)])
        ph.close("vec")
        for c in range(NCH):
            t0 = b * T + c * CH
            gi = b * NCH + c
            if gi == 0:
                ph.load(xnc[:, :, :], xn[:, t0:t0 + CH].rearrange("(k p) t -> p k t", p=128))
            ph.close("ld")
            for m in range(32):
                for k in range(KD):
                    ph.op("tensor", lambda eng, m=m, k=k: eng.matmul(
                        ps[:, m % 4, :], win[:, k, m * 128:(m + 1) * 128], xnc[:, k, :],
                        start=(k == 0), stop=(k == KD - 1)),
                        inc=[("mm", 1)] if k == KD - 1 else (),
                        wait=[("sp", ph.c("sp") - (3 if m > 0 else 0))] if k == 0 else ())
                if m < 16:
                    ph.op("scalar", lambda eng, m=m: eng.copy(xin_c[:, m, 3:3 + CH], ps[:, m % 4, :]),
                          inc=[("sp", 1)],
                          wait=[("mm", ph.c("mm"))] + ([("vec", ph.c("vec"))] if m == 0 else []))
                else:
                    ph.op("scalar", lambda eng, m=m: eng.activation(
                        szc[:, m % 2, :], ps[:, m % 4, :], AF.Silu),
                        inc=[("sp", 1)], wait=[("mm", ph.c("mm")), ("st", ph.c("st"))])
                    if m % 2 == 1:
                        ph.store(sz_o[(m - 17) * 128:(m - 15) * 128, t0:t0 + CH].rearrange(
                            "(k p) t -> p k t", p=128), szc[:, :, :],
                            wait=[("sp", ph.c("sp"))])
            if gi + 1 < NB * NCH:
                t1 = b * T + (c + 1) * CH if c + 1 < NCH else (b + 1) * T
                ph.load(xnc[:, :, :], xn[:, t1:t1 + CH].rearrange("(k p) t -> p k t", p=128),
                        wait=[("mm", ph.c("mm"))])
            ph.close("sp")
            for kt in range(KDI):
                for j in range(4):
                    ph.op("tensor", lambda eng, kt=kt, j=j: eng.matmul(
                        ps2[:, kt % 4, :], dwd[:, j * 16 + kt, :], xin_c[:, kt, j:j + CH],
                        start=(j == 0), stop=(j == 3)),
                        inc=[("mm", 1)] if j == 3 else (),
                        wait=[("sp", ph.c("sp") - (3 if kt > 0 else 0))] if j == 0 else ())
                ph.op("scalar", lambda eng, kt=kt: eng.copy(cvo[:, kt, :], ps2[:, kt % 4, :]),
                      inc=[("sp", 1)], wait=[("mm", ph.c("mm"))])
            ph.close("sp")
            for m in range(KDI):
                for k in range(KDI):
                    ph.op("tensor", lambda eng, m=m, k=k: eng.matmul(
                        ps[:, m % 4, :], pw[:, k, m * 128:(m + 1) * 128], cvo[:, k, :],
                        start=(k == 0), stop=(k == KDI - 1)),
                        inc=[("mm", 1)] if k == KDI - 1 else (),
                        wait=[("sp", ph.c("sp") - 3)] if k == 0 else ())
                ph.op("scalar", lambda eng, m=m: eng.activation(
                    xcc[:, m, :], ps[:, m % 4, :], AF.Silu, bias=cb2[:, m:m + 1], scale=np.float32(1.0)),
                    inc=[("sp", 1)],
                    wait=[("mm", ph.c("mm"))] + ([("st", ph.c("st"))] if m == 0 else []))
            ph.close("sp")
            ph.store(xc_o[:, t0:t0 + CH].rearrange("(k p) t -> p k t", p=128), xcc[:, :, :])
            for k in range(KDI):
                ph.op("tensor", lambda eng, k=k: eng.matmul(
                    ps2[0:64, 0, :], wx[:, k, 0:64], xcc[:, k, :],
                    start=(k == 0), stop=(k == KDI - 1)),
                    wait=[("sp", ph.c("sp"))] if k == 0 else ())
            for k in range(KDI):
                ph.op("tensor", lambda eng, k=k: eng.matmul(
                    ps2[0:8, 1, :], wx[:, k, 64:72], xcc[:, k, :],
                    start=(k == 0), stop=(k == KDI - 1)))
            for k in range(KDI):
                ph.op("tensor", lambda eng, k=k: eng.matmul(
                    ps[0:8, 0, :], wx[:, k, 72:80], xcc[:, k, :],
                    start=(k == 0), stop=(k == KDI - 1)),
                    inc=[("mm", 1)] if k == KDI - 1 else ())
            ph.close("mm")
            ph.op("scalar", lambda eng: eng.copy(dlrc[:], ps2[0:64, 0, :]), wait=[("st", ph.c("st"))])
            ph.op("scalar", lambda eng: eng.copy(bbc[:], ps2[0:8, 1, :]))
            ph.op("scalar", lambda eng: eng.copy(ccc[:], ps[0:8, 0, :]), inc=[("sp", 1)])
            ph.close("sp")
            ph.store(dlr_o[:, t0:t0 + CH], dlrc[:])
            ph.store(bc_o[0:8, t0:t0 + CH], bbc[:])
            ph.store(bc_o[8:16, t0:t0 + CH], ccc[:])
            ph.op("vector", lambda eng: eng.tensor_copy(xin_c[:, :, 0:3], xin_c[:, :, CH:CH + 3]),
                  inc=[("vec", 1)], wait=[("mm", ph.c("mm"))])
            ph.close("vec")


def build_scan(ph, e, dram):
    nc = ph.nc
    xc_i, dlr_i, bc_i, sz_i = dram("xc"), dram("dlr"), dram("bc"), dram("sz")
    ywz_o = dram("ywz")
    wdt = wload_small_bf16(ph, e, dram("wdt"), 64, DI, "wdt")
    eself = e(nc.sbuf_tensor("esel_f", [16, 16, 128], F32))
    eselb = e(nc.sbuf_tensor("esel_b", [16, 16, 128], BF16))
    ph.load(eself[:], dram("esel").rearrange("i (k p) -> k i p", k=16))
    ph.op("vector", lambda eng: eng.tensor_copy(eselb[:], eself[:]),
          inc=[("wcv", 1)], wait=[("ld", ph.c("ld"))])
    bdt2 = e(nc.sbuf_tensor("bdt2_sb", [128, KDI], F32))
    dvec = e(nc.sbuf_tensor("dvec_sb", [128, KDI], F32))
    alog = e(nc.sbuf_tensor("alog_sb", [128, KDI, DS], F32))
    a_mat = e(nc.sbuf_tensor("a_sb", [128, KDI, DS], F32))
    ph.load(bdt2[:], dram("bdt2"))
    ph.load(dvec[:], dram("dvec"))
    ph.load(alog[:], dram("alog").rearrange("(k p) n -> p k n", p=128))
    ph.op("scalar", lambda eng: eng.activation(a_mat[:], alog[:], AF.Exp),
          inc=[("sp", 1)], wait=[("ld", ph.c("ld"))])
    ph.op("vector", lambda eng: eng.tensor_scalar(a_mat[:], a_mat[:], -1.0, None, ALU.mult),
          inc=[("vec", 1)], wait=[("sp", ph.c("sp"))])
    ph.close("vec", "wcv")

    dlr_b = e(nc.sbuf_tensor("sc_dlr", [64, T], BF16))
    bcb = e(nc.sbuf_tensor("sc_bcb", [16, T], BF16))
    bb = e(nc.sbuf_tensor("sc_bb", [128, 16, T], BF16))
    xct = e(nc.sbuf_tensor("sc_xct", [128, 2, T], BF16))
    szt = e(nc.sbuf_tensor("sc_szt", [128, 2, T], BF16))
    delta = e(nc.sbuf_tensor("sc_delta", [128, 2, T], BF16))
    te_b = e(nc.sbuf_tensor("sc_te", [128, T], F32))
    du = e(nc.sbuf_tensor("sc_du", [128, T], BF16))
    a_b = e(nc.sbuf_tensor("sc_a", [128, 4, T], BF16))
    u_b = e(nc.sbuf_tensor("sc_u", [128, T], BF16))
    h_b = u_b
    p_b = e(nc.sbuf_tensor("sc_p", [128, 2, T], BF16))
    y_b = e(nc.sbuf_tensor("sc_y", [128, T], BF16))
    yw = e(nc.sbuf_tensor("sc_yw", [128, 2, T], BF16))
    ps_d = e(nc.psum_tensor("sc_psd", [128, T], F32))
    ps_b = e(nc.psum_tensor("sc_psb", [128, T], F32))

    def sched_chain(dt, sp_exact=False):
        for c in range(NCH):
            if c == 0:
                w0 = [("sp", ph.c("sp") if sp_exact else ph.c("sp") - 8)]
            else:
                w0 = ()
            ph.op("tensor", lambda eng, dt=dt, c=c: eng.matmul(
                ps_d[:, c * CH:(c + 1) * CH], wdt[:, dt * 128:(dt + 1) * 128],
                dlr_b[:, c * CH:(c + 1) * CH], start=True, stop=True),
                inc=[("mm", 1)] if c == NCH - 1 else (),
                wait=w0)
        ph.op("scalar", lambda eng, dt=dt: eng.activation(
            te_b[:], ps_d[:], AF.Exp, bias=bdt2[:, dt:dt + 1], scale=np.float32(1.0)),
            wait=[("mm", ph.c("mm")), ("vec", ph.c("vec") - 10)])
        ph.op("scalar", lambda eng, dt=dt: eng.activation(
            delta[:, dt % 2, :], te_b[:], AF.Ln, bias=np.float32(1.0)),
            inc=[("sp", 1)])

    def agen(dt, n, tb):
        ph.op("scalar", lambda eng, dt=dt, n=n: eng.activation(
            a_b[:, n % 4, :], delta[:, dt % 2, :], AF.Exp, scale=a_mat[:, dt, n:n + 1]),
            inc=[("sp", 1)], wait=[("tts", tb + n - 3)] if tb + n - 3 > 0 else ())

    BORD = [0, 8, 1, 9, 2, 10, 3, 11, 4, 12, 5, 13, 6, 14, 7, 15]
    for b in range(NB):
        ph.load(dlr_b[:], dlr_i[:, b * T:(b + 1) * T], wait=[("vec", ph.c("vec"))])
        ph.load(bcb[:], bc_i[:, b * T:(b + 1) * T])
        ph.load(xct[:, 0, :], xc_i[0:128, b * T:(b + 1) * T])
        ph.load(szt[:, 0, :], sz_i[0:128, b * T:(b + 1) * T])
        ph.close("ld")
        sched_chain(0)
        sLn = ph.c("sp")
        tb0 = ph.c("tts")
        sAg = {}
        sEv = {}
        for n in range(4):
            agen(0, n, tb0)
            sAg[n] = ph.c("sp")
        for j, i in enumerate(BORD):
            if j >= 8 and j % 2 == 0:
                n = 4 + (j - 8) // 2
                agen(0, n, tb0)
                sAg[n] = ph.c("sp")
            pst = ps_b if j % 2 == 0 else ps_d
            for c in range(NCH):
                ph.op("tensor", lambda eng, i=i, c=c, pst=pst: eng.matmul(
                    pst[:, c * CH:(c + 1) * CH], eselb[:, i, :],
                    bcb[:, c * CH:(c + 1) * CH], start=True, stop=True),
                    inc=[("mm", 1)] if c == NCH - 1 else (),
                    wait=([("sp", ph.c("sp"))] if j == 0 else [("sp", ph.c("sp") - 1)]) if c == 0 else ())
            ph.op("scalar", lambda eng, i=i, pst=pst: eng.copy(bb[:, i, :], pst[:]),
                  inc=[("sp", 1)], wait=[("mm", ph.c("mm"))])
            sEv[i] = ph.c("sp")
        for dt in range(KDI):
            if dt + 1 < KDI:
                lsem = "ld" if (dt + 1) % 2 == 0 else "ldb"
                ph.load(xct[:, (dt + 1) % 2, :], xc_i[(dt + 1) * 128:(dt + 2) * 128, b * T:(b + 1) * T],
                        wait=[("vec", ph.c("vec") - 2)] if dt >= 1 else (), sem=lsem)
                ph.load(szt[:, (dt + 1) % 2, :], sz_i[(dt + 1) * 128:(dt + 2) * 128, b * T:(b + 1) * T],
                        sem=lsem)
            dsem = "ld" if dt % 2 == 0 else "ldb"
            ph.op("vector", lambda eng, dt=dt: eng.tensor_tensor(du[:], delta[:, dt % 2, :], xct[:, dt % 2, :], ALU.mult),
                  wait=[("sp", sLn if dt == 0 else ph.c("sp")), (dsem, ph.c(dsem))])
            ph.op("vector", lambda eng, dt=dt: eng.tensor_scalar(
                y_b[:], xct[:, dt % 2, :], dvec[:, dt:dt + 1], None, ALU.mult),
                inc=[("vec", 1)])
            tts_base = ph.c("tts")
            for n in range(DS):
                if dt > 0:
                    agen(dt, n, tts_base)
                ph.op("vector", lambda eng, n=n: eng.tensor_tensor(u_b[:], du[:], bb[:, n, :], ALU.mult),
                      wait=[("sp", sEv[n])] if dt == 0 else ())
                ph.op("vector", lambda eng, n=n: eng.tensor_tensor_scan(
                    u_b[:], a_b[:, n % 4, :], u_b[:], 0.0, ALU.mult, ALU.add),
                    inc=[("tts", 1)], wait=[("sp", sAg[n] if dt == 0 else ph.c("sp"))])
                ph.op("vector", lambda eng, n=n: eng.tensor_tensor(
                    p_b[:, n % 2, :], h_b[:], bb[:, 8 + n, :], ALU.mult),
                    wait=[("sp", sEv[8 + n])] if dt == 0 else ())
                ph.op("vector", lambda eng, n=n: eng.tensor_tensor(
                    y_b[:], y_b[:], p_b[:, n % 2, :], ALU.add), inc=[("vec", 1)])
            ssem = "st" if dt % 2 == 0 else "stb"
            ph.op("vector", lambda eng, dt=dt: eng.tensor_tensor(yw[:, dt % 2, :], y_b[:], szt[:, dt % 2, :], ALU.mult),
                  inc=[("vec", 1)], wait=[(ssem, ph.c(ssem))])
            ph.store(ywz_o[dt * 128:(dt + 1) * 128, b * T:(b + 1) * T], yw[:, dt % 2, :],
                     wait=[("vec", ph.c("vec"))], sem=ssem)
            if dt + 1 < KDI:
                sched_chain(dt + 1, sp_exact=(dt == 0))
        ph.close("vec", "st", "stb", "sp", "tts")


def build_l0out(ph, e, dram):
    nc = ph.nc
    ywz_i, x_i, f_i, x1_o = dram("ywz"), dram("x"), dram("ffn0"), dram("x1")
    wout = wload_bf16(ph, e, dram("wout"), KDI, D, "wo")
    yw_sb = e(nc.sbuf_tensor("lo_yw", [128, KDI, T], BF16))
    xs = e(nc.sbuf_tensor("lo_xs", [128, 2, T], F32))
    fs = e(nc.sbuf_tensor("lo_fs", [128, 2, T], F32))
    t_sb = e(nc.sbuf_tensor("lo_t", [128, T], F32))
    o_sb = e(nc.sbuf_tensor("lo_o", [128, T], F32))
    ps = e(nc.psum_tensor("lo_ps", [128, 2, T], F32))
    ph.close("wcv", "ld")
    for b in range(NB):
        ph.load(yw_sb[:, :, :], ywz_i[:, b * T:(b + 1) * T].rearrange("(k p) t -> p k t", p=128))
        ph.close("ld")
        for m in range(KD):
            ph.load(xs[:, m % 2, :], x_i[m * 128:(m + 1) * 128, b * T:(b + 1) * T],
                    wait=[("vec", ph.c("vec"))])
            ph.load(fs[:, m % 2, :], f_i[m * 128:(m + 1) * 128, b * T:(b + 1) * T])
            for c in range(NCH):
                for k in range(KDI):
                    ph.op("tensor", lambda eng, m=m, c=c, k=k: eng.matmul(
                        ps[:, m % 2, c * CH:(c + 1) * CH], wout[:, k, m * 128:(m + 1) * 128],
                        yw_sb[:, k, c * CH:(c + 1) * CH], start=(k == 0), stop=(k == KDI - 1)),
                        inc=[("mm", 1)] if (c == NCH - 1 and k == KDI - 1) else (),
                        wait=[("vec", ph.c("vec") - 2)] if (c == 0 and k == 0 and ph.c("vec") >= 2) else ())
            ph.op("vector", lambda eng, m=m: eng.tensor_tensor(t_sb[:], ps[:, m % 2, :], xs[:, m % 2, :], ALU.add),
                  wait=[("mm", ph.c("mm")), ("ld", ph.c("ld")), ("st", ph.c("st"))])
            ph.op("vector", lambda eng, m=m: eng.tensor_tensor(o_sb[:], t_sb[:], fs[:, m % 2, :], ALU.add),
                  inc=[("vec", 2)])
            ph.store(x1_o[m * 128:(m + 1) * 128, b * T:(b + 1) * T], o_sb[:],
                     wait=[("vec", ph.c("vec"))])
        ph.close("vec", "st")


def build_l1attn(ph, e, dram):
    nc = ph.nc
    xn1_i, at1_o = dram("xn1"), dram("at1")
    wqkv = e(nc.sbuf_tensor("wqkv_bf", [128, KD, 3 * D], BF16))
    wgate = e(nc.sbuf_tensor("wg1_bf", [128, KD, D], BF16))
    wout1 = e(nc.sbuf_tensor("wo1_bf", [128, KD, D], BF16))
    em = e(nc.sbuf_tensor("em_bf", [128, NOFF, 88], BF16))
    smt = e(nc.sbuf_tensor("smt_b", [44, NOFF, 128], BF16))
    gm = e(nc.sbuf_tensor("gm_bf", [88, 44], BF16))
    ehm = e(nc.sbuf_tensor("ehm_bf", [88, 2], BF16))
    ebc = e(nc.sbuf_tensor("ebc_sb", [2, 44], F32))
    pb = e(nc.sbuf_tensor("pb_sb", [88, 8], F32))
    bqkv = e(nc.sbuf_tensor("bqkv_sb", [128, KH], F32))
    bgate = e(nc.sbuf_tensor("bgate_sb", [128, KD], F32))
    bout = e(nc.sbuf_tensor("bout_sb", [128, KD], F32))
    xn1 = e(nc.sbuf_tensor("la_xn1", [128, KD, T], BF16))
    q_sb = e(nc.sbuf_tensor("la_q", [128, T], BF16))
    kpad = e(nc.sbuf_tensor("la_kp", [128, MAXOFF + T], BF16))
    vpad = e(nc.sbuf_tensor("la_vp", [128, MAXOFF + T], BF16))
    prod = e(nc.sbuf_tensor("la_prod", [128, 2, T], BF16))
    eu = prod[0:88, 0, :]
    buf88 = e(nc.sbuf_tensor("la_b88", [44, T], F32))
    lnt = buf88[0:2, :]
    r2 = lnt
    rb44 = buf88[0:44, :]
    ae = prod[0:44, 1, :]
    abc_sb = e(nc.sbuf_tensor("la_abc", [128, 2, T], BF16))
    p1 = q_sb
    acc = e(nc.sbuf_tensor("la_acc", [128, T], BF16))
    g_sb = prod[:, 0, :]
    ghm = e(nc.sbuf_tensor("la_ghm", [128, KD, T], BF16))
    oo = acc
    ps_a = e(nc.psum_tensor("la_psa", [128, T], F32))
    ps_b = e(nc.psum_tensor("la_psb", [128, T], F32))
    with nc.sbuf_tensor("la_wstg", [128, 2, 448], F32) as wstg:
        wload_bf16(ph, e, dram("wqkv"), KD, 3 * D, "wqkv", stg=wstg, dst=wqkv)
        wload_bf16(ph, e, dram("wgate"), KD, D, "wg1", stg=wstg, dst=wgate)
        wload_bf16(ph, e, dram("wout1"), KD, D, "wo1", stg=wstg, dst=wout1)
        wload_bf16(ph, e, dram("Emat"), NOFF, 88, "em", stg=wstg, dst=em)
        smt2 = smt[:].rearrange("p a q -> p (a q)")
        for c0 in range(0, NOFF * 128, 448):
            cw = min(448, NOFF * 128 - c0)
            ph.load(wstg[0:44, 0, 0:cw], dram("Smat")[:, c0:c0 + cw],
                    wait=[("wcv", ph.c("wcv"))])
            ph.op("vector", lambda eng, c0=c0, cw=cw: eng.tensor_copy(
                smt2[:, c0:c0 + cw], wstg[0:44, 0, 0:cw]),
                inc=[("wcv", 1)], wait=[("ld", ph.c("ld"))])
        ph.load(wstg[0:88, 1, 0:44], dram("Gmat"), wait=[("wcv", ph.c("wcv"))])
        ph.op("vector", lambda eng: eng.tensor_copy(gm[:], wstg[0:88, 1, 0:44]),
              inc=[("wcv", 1)], wait=[("ld", ph.c("ld"))])
        ph.load(wstg[0:88, 0, 0:2], dram("Ehmat"), wait=[("wcv", ph.c("wcv"))])
        ph.op("vector", lambda eng: eng.tensor_copy(ehm[:], wstg[0:88, 0, 0:2]),
              inc=[("wcv", 1)], wait=[("ld", ph.c("ld"))])
        ph.load(ebc[:], dram("Ebc"))
        ph.load(pb[:], dram("pb88"))
        ph.load(bqkv[:], dram("bqkv2"))
        ph.load(bgate[:], dram("bgate2"))
        ph.load(bout[:], dram("bout2"))
        ph.close("wcv", "ld")

    ph.op("vector", lambda eng: eng.memset(kpad[:, 0:MAXOFF], 0.0))
    ph.op("vector", lambda eng: eng.memset(vpad[:, 0:MAXOFF], 0.0), inc=[("vec", 1)])
    ph.close("vec", "wcv", "ld")

    def mm8(wtile, mcol, rhs, dst_note, wait0, pst=None):
        if pst is None:
            pst = ps_a
        for c in range(NCH):
            for k in range(KD):
                ph.op("tensor", lambda eng, c=c, k=k, wtile=wtile, mcol=mcol, rhs=rhs, pst=pst: eng.matmul(
                    pst[:, c * CH:(c + 1) * CH], wtile[:, k, mcol * 128:(mcol + 1) * 128],
                    rhs[:, k, c * CH:(c + 1) * CH], start=(k == 0), stop=(k == KD - 1)),
                    inc=[("mm", 1)] if (c == NCH - 1 and k == KD - 1) else (),
                    wait=wait0 if (c == 0 and k == 0) else ())

    for b in range(NB):
        if b == 0:
            ph.load(xn1[:, :, :], xn1_i[:, 0:T].rearrange("(k p) t -> p k t", p=128))
            ph.close("ld")
        for hp in range(8):
            q_sp = ph.c("sp")
            mm8(wqkv, hp, xn1, "q",
                [("sp", ph.c("sp"))] + ([("ld", ph.c("ld"))] if hp == 0 else []))
            mm8(wqkv, 8 + hp, xn1, "k", [("sp", ph.c("sp"))], pst=ps_b)
            ph.op("scalar", lambda eng, hp=hp: eng.activation(
                q_sb[:], ps_a[:], AF.Identity, bias=bqkv[:, hp:hp + 1], scale=np.float32(1.0)),
                inc=[("sp", 1)], wait=[("mm", ph.c("mm") - 1), ("vec", ph.c("vec"))])
            mm8(wqkv, 16 + hp, xn1, "v", [("sp", ph.c("sp"))])
            ph.op("scalar", lambda eng, hp=hp: eng.activation(
                kpad[:, MAXOFF:], ps_b[:], AF.Identity, bias=bqkv[:, 8 + hp:9 + hp], scale=np.float32(1.0)),
                inc=[("sp", 1)], wait=[("mm", ph.c("mm") - 1)])
            ph.op("scalar", lambda eng, hp=hp: eng.activation(
                vpad[:, MAXOFF:], ps_a[:], AF.Identity, bias=bqkv[:, 16 + hp:17 + hp], scale=np.float32(1.0)),
                inc=[("sp", 1)], wait=[("mm", ph.c("mm"))])
            ph.close("sp")
            mm_base = ph.c("mm")
            for oi, off in enumerate(DOFFS):
                ph.op("vector", lambda eng, oi=oi, off=off: eng.tensor_tensor(
                    prod[:, oi % 2, :], q_sb[:], kpad[:, MAXOFF - off:MAXOFF - off + T], ALU.mult),
                    inc=[("vec", 1)],
                    wait=[("mm", mm_base + oi - 1)] if oi >= 2 else [("mm", mm_base)])
                for c in range(NCH):
                    ph.op("tensor", lambda eng, oi=oi, c=c: eng.matmul(
                        ps_b[0:88, c * CH:(c + 1) * CH], em[:, oi, :],
                        prod[:, oi % 2, c * CH:(c + 1) * CH], start=(oi == 0), stop=(oi == NOFF - 1)),
                        inc=[("mm", 1)] if c == NCH - 1 else (),
                        wait=[("vec", ph.c("vec"))] if c == 0 else ())
            ph.op("scalar", lambda eng, hp=hp: eng.activation(
                eu[:], ps_b[0:88, :], AF.Exp, bias=pb[:, hp:hp + 1], scale=np.float32(0.125)),
                inc=[("sp", 1)], wait=[("mm", ph.c("mm"))])
            ph.close("sp")

            def gate_chunk(c, hp=hp):
                for k in range(KD):
                    ph.op("tensor", lambda eng, c=c, k=k, hp=hp: eng.matmul(
                        ps_b[:, c * CH:(c + 1) * CH], wgate[:, k, hp * 128:(hp + 1) * 128],
                        xn1[:, k, c * CH:(c + 1) * CH], start=(k == 0), stop=(k == KD - 1)),
                        inc=[("mm", 1)] if k == KD - 1 else (),
                        wait=[("sp", ph.c("sp"))] if (c == 0 and k == 0) else ())

            for c in range(NCH):
                ph.op("tensor", lambda eng, c=c: eng.matmul(
                    ps_a[0:2, c * CH:(c + 1) * CH], ehm[:], eu[:, c * CH:(c + 1) * CH],
                    start=True, stop=True),
                    inc=[("mm", 1)] if c == NCH - 1 else ())
            gate_chunk(0)
            ph.op("scalar", lambda eng: eng.activation(lnt[:], ps_a[0:2, :], AF.Ln),
                  wait=[("mm", ph.c("mm") - 1)])
            ph.op("scalar", lambda eng: eng.activation(r2[:], lnt[:], AF.Exp, scale=np.float32(-1.0)),
                  inc=[("sp", 1)])
            gate_chunk(1)
            for c in range(NCH):
                ph.op("tensor", lambda eng, c=c: eng.matmul(
                    ps_a[0:44, c * CH:(c + 1) * CH], ebc[:], r2[:, c * CH:(c + 1) * CH],
                    start=True, stop=True),
                    inc=[("mm", 1)] if c == NCH - 1 else (),
                    wait=[("sp", ph.c("sp"))] if c == 0 else ())
            gate_chunk(2)
            ph.op("scalar", lambda eng: eng.copy(rb44[:], ps_a[0:44, :]),
                  inc=[("sp", 1)], wait=[("mm", ph.c("mm") - 1)])
            for c in range(NCH):
                ph.op("tensor", lambda eng, c=c: eng.matmul(
                    ps_a[64:108, c * CH:(c + 1) * CH], gm[:], eu[:, c * CH:(c + 1) * CH],
                    start=True, stop=True),
                    inc=[("mm", 1)] if c == NCH - 1 else ())
            gate_chunk(3)
            ph.op("vector", lambda eng: eng.tensor_tensor(ae[:], ps_a[64:108, :], rb44[:], ALU.mult),
                  inc=[("vec", 1)], wait=[("mm", ph.c("mm") - 1), ("sp", ph.c("sp"))])
            ph.op("scalar", lambda eng, hp=hp: eng.activation(
                g_sb[:], ps_b[:], AF.Sigmoid, bias=bgate[:, hp:hp + 1], scale=np.float32(1.0)),
                inc=[("gs", 1)], wait=[("mm", ph.c("mm")), ("vec", ph.c("vec"))])
            ph.close("vec")
            vb = ph.c("vec")
            for oi, off in enumerate(DOFFS):
                pp = ps_b if oi % 2 == 0 else ps_a
                for c in range(NCH):
                    ph.op("tensor", lambda eng, oi=oi, c=c, pp=pp: eng.matmul(
                        pp[:, c * CH:(c + 1) * CH], smt[:, oi, :],
                        ae[:, c * CH:(c + 1) * CH], start=True, stop=True),
                        inc=[("mm", 1)] if c == NCH - 1 else (),
                        wait=([("vec", vb), ("gs", ph.c("gs"))] if oi < 2
                              else [("sp", ph.c("sp") - 1)]) if c == 0 else ())
                ph.op("scalar", lambda eng, oi=oi, pp=pp: eng.copy(abc_sb[:, oi % 2, :], pp[:]),
                      inc=[("sp", 1)],
                      wait=[("mm", ph.c("mm"))] + ([("vec", ph.c("vec") - 1)] if oi >= 2 else []))
                ph.op("vector", lambda eng, oi=oi, off=off: eng.tensor_tensor(
                    p1[:], vpad[:, MAXOFF - off:MAXOFF - off + T], abc_sb[:, oi % 2, :], ALU.mult),
                    wait=[("sp", ph.c("sp"))])
                if oi == 0:
                    ph.op("vector", lambda eng: eng.tensor_copy(acc[:], p1[:]), inc=[("vec", 1)])
                else:
                    ph.op("vector", lambda eng: eng.tensor_tensor(acc[:], acc[:], p1[:], ALU.add),
                          inc=[("vec", 1)])
            ph.op("vector", lambda eng, hp=hp: eng.tensor_tensor(ghm[:, hp, :], acc[:], g_sb[:], ALU.mult),
                  inc=[("vec", 1)], wait=[("gs", ph.c("gs"))])
            ph.close("vec", "sp")
        if b + 1 < NB:
            ph.load(xn1[:, :, :], xn1_i[:, (b + 1) * T:(b + 2) * T].rearrange("(k p) t -> p k t", p=128),
                    wait=[("mm", ph.c("mm"))])
        for m in range(KD):
            pst = ps_a if m % 2 == 0 else ps_b
            mm8(wout1, m, ghm, "o",
                [("sp", ph.c("sp") - (1 if m >= 2 else 0)), ("st", ph.c("st"))], pst=pst)
            ph.op("scalar", lambda eng, m=m, pst=pst: eng.activation(
                oo[:], pst[:], AF.Identity, bias=bout[:, m:m + 1], scale=np.float32(1.0)),
                inc=[("sp", 1)], wait=[("mm", ph.c("mm")), ("st", ph.c("st"))])
            ph.store(at1_o[m * 128:(m + 1) * 128, b * T:(b + 1) * T], oo[:],
                     wait=[("sp", ph.c("sp"))])
        ph.close("sp", "st")


def build_l1ffn(ph, e, dram):
    nc = ph.nc
    xn1_i, x1_i, at1_i, h_s, out_o = dram("xn1"), dram("x1"), dram("at1"), dram("hmid"), dram("outx")
    with ExitStack() as we:
        ee = we.enter_context
        wg = wload_bf16_i32(ph, ee, dram("wgT"), KD, HID, "wg")
        wu = wload_bf16_i32(ph, ee, dram("wuT"), KD, HID, "wu")
        xn1 = ee(nc.sbuf_tensor("lf_xn1", [128, KD, T], BF16))
        sg = ee(nc.sbuf_tensor("lf_sg", [128, T], BF16))
        hh = ee(nc.sbuf_tensor("lf_h", [128, T], BF16))
        ps_g = ee(nc.psum_tensor("lf_psg", [128, 2, T // 2], F32))
        ps_u = ee(nc.psum_tensor("lf_psu", [128, 2, T // 2], F32))
        ph.close("wcv", "ld")
        for b in range(NB):
            ph.load(xn1[:, :, :], xn1_i[:, b * T:(b + 1) * T].rearrange("(k p) t -> p k t", p=128))
            ph.close("ld")
            for hm in range(KH):
                for half in range(2):
                    HB = T // 2
                    for c in range(2):
                        for k in range(KD):
                            ph.op("tensor", lambda eng, hm=hm, half=half, c=c, k=k: eng.matmul(
                                ps_g[:, half, c * CH:(c + 1) * CH], wg[:, k, hm * 128:(hm + 1) * 128],
                                xn1[:, k, half * HB + c * CH:half * HB + (c + 1) * CH],
                                start=(k == 0), stop=(k == KD - 1)),
                                wait=[("sp", ph.c("sp") - 1)] if (c == 0 and k == 0) else ())
                    for c in range(2):
                        for k in range(KD):
                            ph.op("tensor", lambda eng, hm=hm, half=half, c=c, k=k: eng.matmul(
                                ps_u[:, half, c * CH:(c + 1) * CH], wu[:, k, hm * 128:(hm + 1) * 128],
                                xn1[:, k, half * HB + c * CH:half * HB + (c + 1) * CH],
                                start=(k == 0), stop=(k == KD - 1)),
                                inc=[("mm", 1)] if (c == 1 and k == KD - 1) else (),
                                wait=[("vec", ph.c("vec") - 1)] if (c == 0 and k == 0) else ())
                    ph.op("scalar", lambda eng, half=half: eng.activation(
                        sg[:, half * HB:(half + 1) * HB], ps_g[:, half, :], AF.Silu,
                        scale=np.float32(1.0 / 127.0)),
                        inc=[("sp", 1)], wait=[("mm", ph.c("mm"))])
                    ph.op("vector", lambda eng, half=half: eng.tensor_tensor(
                        hh[:, half * HB:(half + 1) * HB], ps_u[:, half, :], sg[:, half * HB:(half + 1) * HB],
                        ALU.mult),
                        inc=[("vec", 1)], wait=[("sp", ph.c("sp")), ("st", ph.c("st"))])
                ph.store(h_s[hm * 128:(hm + 1) * 128, b * T:(b + 1) * T], hh[:],
                         wait=[("vec", ph.c("vec"))])
            ph.close("st", "vec", "sp")
    wd = wload_bf16_i32(ph, e, dram("wdT"), KH, D, "wd")
    hb = e(nc.sbuf_tensor("lf_hb", [128, KH, T], BF16))
    xs = e(nc.sbuf_tensor("lf_xs", [128, 2, T], F32))
    as_ = e(nc.sbuf_tensor("lf_as", [128, 2, T], BF16))
    d_sb = e(nc.sbuf_tensor("lf_d", [128, T], F32))
    t_sb = e(nc.sbuf_tensor("lf_t", [128, T], F32))
    o_sb = d_sb
    ps_d = e(nc.psum_tensor("lf_psd", [128, 2, T], F32))
    ph.close("wcv", "ld", "st")
    for b in range(NB):
        ph.load(hb[:, :, :], h_s[:, b * T:(b + 1) * T].rearrange("(k p) t -> p k t", p=128))
        ph.close("ld")
        for m in range(KD):
            ph.load(xs[:, m % 2, :], x1_i[m * 128:(m + 1) * 128, b * T:(b + 1) * T],
                    wait=[("vec", ph.c("vec"))])
            ph.load(as_[:, m % 2, :], at1_i[m * 128:(m + 1) * 128, b * T:(b + 1) * T])
            for c in range(NCH):
                for k in range(KH):
                    ph.op("tensor", lambda eng, m=m, c=c, k=k: eng.matmul(
                        ps_d[:, m % 2, c * CH:(c + 1) * CH], wd[:, k, m * 128:(m + 1) * 128],
                        hb[:, k, c * CH:(c + 1) * CH], start=(k == 0), stop=(k == KH - 1)),
                        inc=[("mm", 1)] if (c == NCH - 1 and k == KH - 1) else (),
                        wait=[("sp", ph.c("sp") - 1)] if (c == 0 and k == 0) else ())
            ph.op("scalar", lambda eng, m=m: eng.activation(d_sb[:], ps_d[:, m % 2, :], AF.Identity, scale=np.float32(QS2)),
                  inc=[("sp", 1)], wait=[("mm", ph.c("mm")), ("vec", ph.c("vec")), ("st", ph.c("st"))])
            ph.op("vector", lambda eng, m=m: eng.tensor_tensor(t_sb[:], d_sb[:], xs[:, m % 2, :], ALU.add),
                  wait=[("sp", ph.c("sp")), ("ld", ph.c("ld")), ("st", ph.c("st"))])
            ph.op("vector", lambda eng, m=m: eng.tensor_tensor(o_sb[:], t_sb[:], as_[:, m % 2, :], ALU.add),
                  inc=[("vec", 2)])
            ph.store(out_o[m * 128:(m + 1) * 128, b * T:(b + 1) * T], o_sb[:],
                     wait=[("vec", ph.c("vec"))])
        ph.close("vec", "st")


PHASES = [
    ("ln0", lambda ph, e, dram: build_ln(ph, e, dram, "x", "xn"),
     ["x"], ["xn"]),
    ("l0ffn", build_l0ffn, ["xn", "wrT", "wkT", "wvT", "mur", "muk"], ["ffn0"]),
    ("mambapre", build_mamba_pre,
     ["xn", "w_in", "dwdiag", "pw", "cb2", "wx"], ["sz", "xc", "dlr", "bc"]),
    ("scan", build_scan,
     ["xc", "dlr", "bc", "sz", "wdt", "bdt2", "alog", "dvec", "esel"], ["ywz"]),
    ("l0out", build_l0out, ["ywz", "x", "ffn0", "wout"], ["x1"]),
    ("ln1", lambda ph, e, dram: build_ln(ph, e, dram, "x1", "xn1"),
     ["x1"], ["xn1"]),
    ("l1attn", build_l1attn,
     ["xn1", "wqkv", "bqkv2", "wgate", "bgate2", "wout1", "bout2", "pb88",
      "Emat", "Gmat", "Smat", "Ehmat", "Ebc"], ["at1"]),
    ("l1ffn", build_l1ffn, ["xn1", "x1", "at1", "wgT", "wuT", "wdT"], ["outx"]),
]


def build_phase_prog(name, builder, ins, outs):
    nc = bass.Bass("TRN2", target_bir_lowering=False)
    ctx = ExitStack()
    aps = {}
    for n in ins:
        shape, dt = WSPEC[n] if n in WSPEC else SCRATCH[n]
        aps[n] = nc.dram_tensor(n, list(shape), dt, kind="ExternalInput").ap()
    for n in outs:
        shape, dt = SCRATCH[n]
        aps[n] = nc.dram_tensor(n, list(shape), dt, kind="ExternalOutput").ap()
    if name == "l1ffn":
        shape, dt = SCRATCH["hmid"]
        aps["hmid"] = nc.dram_tensor("hmid", list(shape), dt).ap()
    with ctx:
        ph = Ph(nc, ctx, name)
        builder(ph, ctx.enter_context, lambda n: aps[n])
        ph.finalize()
        ph.emit()
    return nc


class NCProxy:

    def __init__(self, nc, prefix):
        self._nc = nc
        self._prefix = prefix

    def __getattr__(self, a):
        return getattr(self._nc, a)

    def sbuf_tensor(self, name, *a, **k):
        return self._nc.sbuf_tensor(self._prefix + name, *a, **k)

    def psum_tensor(self, name, *a, **k):
        return self._nc.psum_tensor(self._prefix + name, *a, **k)


def build_fused():
    nc = bass.Bass("TRN2", target_bir_lowering=False)
    aps = {}
    for n, (shape, dt) in WSPEC.items():
        aps[n] = nc.dram_tensor(n, list(shape), dt, kind="ExternalInput").ap()
    for n, (shape, dt) in SCRATCH.items():
        if n == "outx":
            aps[n] = nc.dram_tensor(n, list(shape), dt, kind="ExternalOutput").ap()
        else:
            aps[n] = nc.dram_tensor(n, list(shape), dt).ap()
    ctx = ExitStack()
    with ctx:
        ph = Ph(nc, ctx, "mk")
        for pi, (name, builder, ins, outs) in enumerate(PHASES):
            ph.nc = NCProxy(nc, f"p{pi}_")
            with ExitStack() as pctx:
                builder(ph, pctx.enter_context, lambda n: aps[n])
                ph.close(*list(ph.cnt.keys()))
        ph.nc = nc
        ph.finalize()
        ph.emit()
    return nc


_PROG_CACHE = {}


def kernel(**inputs):
    w = host_prep(inputs)
    x = inputs["L0_attn_x"]
    state = []
    for c in range(NCORES):
        xs = np.ascontiguousarray(
            x[c * NB:(c + 1) * NB].transpose(2, 0, 1).reshape(D, BT))
        state.append({"x": xs, **w})

    if MODE == "fused":
        if "fused" not in _PROG_CACHE:
            _PROG_CACHE["fused"] = build_fused()
        nc = _PROG_CACHE["fused"]
        in_maps = [dict(state[c]) for c in range(NCORES)]
        res = run_bass_kernel_spmd(nc, in_maps, core_ids=list(range(NCORES)))
        for c in range(NCORES):
            state[c]["outx"] = np.asarray(res.results[c]["outx"])
    else:
        for name, builder, ins, outs in PHASES:
            key = name
            if key not in _PROG_CACHE:
                _PROG_CACHE[key] = build_phase_prog(name, builder, ins, outs)
            nc = _PROG_CACHE[key]
            in_maps = [{n: state[c][n] for n in ins} for c in range(NCORES)]
            res = run_bass_kernel_spmd(nc, in_maps, core_ids=list(range(NCORES)))
            for c in range(NCORES):
                for n in outs:
                    state[c][n] = np.asarray(res.results[c][n])

    out = np.empty((32, T, D), np.float32)
    for c in range(NCORES):
        o = state[c]["outx"].reshape(D, NB, T).transpose(1, 2, 0)
        out[c * NB:(c + 1) * NB] = o
    return out


# version 24 (final)
# speedup vs baseline: 1.2021x; 1.0434x over previous
import numpy as np
from contextlib import ExitStack
import concourse.bass as bass
import concourse.mybir as mybir
from concourse.bass_utils import run_bass_kernel_spmd

F32 = mybir.dt.float32
BF16 = mybir.dt.bfloat16
I32 = mybir.dt.int32
AF = mybir.ActivationFunctionType
ALU = mybir.AluOpType

NCORES = 8
NB = 4
T = 2048
BT = NB * T
D = 1024
DI = 2048
DS = 8
HID = 3072
KD = D // 128
KDI = DI // 128
KH = HID // 128
CH = 512
NCH = T // CH
EPS = 1e-5
QS2 = 1.0 / (127.0 * 127.0)

N_SCALES, N_TAPS = 11, 4
OFFSETS = [(1 << j) * tau for j in range(N_SCALES) for tau in range(N_TAPS)]
DOFFS = sorted({o for o in OFFSETS if o < T})
NOFF = len(DOFFS)
MAXOFF = 1536

MODE = "fused"

ENGINES = ("sync", "tensor", "vector", "scalar", "gpsimd")


class Ph:

    def __init__(self, nc, ctx, tag):
        self.nc, self.ctx, self.tag = nc, ctx, tag
        self.q = {k: [] for k in ENGINES}
        self._sems = {}
        self.cnt = {}
        self.entry = []
        self._seen = set()

    def sem(self, name):
        if name not in self._sems:
            self._sems[name] = self.ctx.enter_context(
                self.nc.semaphore(f"{self.tag}_{name}"))
            self.cnt[name] = 0
        return self._sems[name]

    def c(self, name):
        self.sem(name)
        return self.cnt[name]

    def op(self, eng, fn, inc=(), wait=()):
        w = []
        if eng not in self._seen:
            self._seen.add(eng)
            w += [(self._sems[n], v) for n, v in self.entry]
        w += [(self.sem(n), v) for n, v in wait]
        incs = []
        for n, k in inc:
            self.sem(n)
            self.cnt[n] += k
            incs.append((self._sems[n], k))
        self.q[eng].append((w, fn, incs))

    def close(self, *semnames):
        for n in semnames:
            self.sem(n)
        self.entry = [(n, self.cnt[n]) for n in semnames]
        self._seen = set()

    def load(self, dst, src, wait=(), sem="ld"):
        self.op("gpsimd", lambda e, d=dst, s=src: e.dma_start(out=d, in_=s),
                inc=[(sem, 16)], wait=wait)

    def store(self, dst, src, wait=(), sem="st"):
        self.op("gpsimd", lambda e, d=dst, s=src: e.dma_start(out=d, in_=s),
                inc=[(sem, 16)], wait=wait)

    def finalize(self):
        for s in ("st", "stb"):
            if s in self.cnt and self.cnt[s]:
                self.op("gpsimd", None, wait=[(s, self.c(s))])

    def emit(self):
        block = self.ctx.enter_context(self.nc.Block())
        for name in ENGINES:
            items = self.q[name]
            if not items:
                continue

            def body(engine, items=items):
                for w, fn, incs in items:
                    for s, v in w:
                        if v > 0:
                            engine.wait_ge(s, v)
                    inst = fn(engine) if fn is not None else None
                    for s, k in incs:
                        inst = inst.then_inc(s, k)
            getattr(block, name)(body)
        self.q = {k: [] for k in ENGINES}



def host_prep(inp):
    w = {}
    w["w_in"] = np.ascontiguousarray(inp["L0_attn_W_in"])
    dw = inp["L0_attn_dw_kernel"]
    dwd = np.zeros((4, KDI, 128, 128), np.float32)
    for j in range(4):
        for kt in range(KDI):
            np.fill_diagonal(dwd[j, kt], dw[j, kt * 128:(kt + 1) * 128])
    w["dwdiag"] = dwd.reshape(4 * KDI * 128, 128)
    w["pw"] = np.ascontiguousarray(inp["L0_attn_pw_kernel"])
    w["cb2"] = np.ascontiguousarray(inp["L0_attn_conv_bias"].reshape(KDI, 128).T)
    w["wx"] = np.ascontiguousarray(inp["L0_attn_W_x"])
    w["wdt"] = np.ascontiguousarray(inp["L0_attn_W_dt"])
    w["bdt2"] = np.ascontiguousarray(inp["L0_attn_b_dt"].reshape(KDI, 128).T)
    w["alog"] = np.ascontiguousarray(inp["L0_attn_A_log"])
    w["dvec"] = np.ascontiguousarray(inp["L0_attn_D"].reshape(KDI, 128).T)
    w["wout"] = np.ascontiguousarray(inp["L0_attn_W_out"])
    w["wrT"] = np.ascontiguousarray(inp["L0_ffn_W_r"].T)
    w["wkT"] = np.ascontiguousarray(inp["L0_ffn_W_k"].T)
    w["wvT"] = np.ascontiguousarray(inp["L0_ffn_W_v"].T)
    w["mur"] = np.ascontiguousarray(inp["L0_ffn_mu_r"].reshape(KD, 128).T)
    w["muk"] = np.ascontiguousarray(inp["L0_ffn_mu_k"].reshape(KD, 128).T)
    esel = np.zeros((16, 16, 128), np.float32)
    for i in range(16):
        esel[i, i, :] = 1.0
    w["esel"] = esel
    w["wqkv"] = np.ascontiguousarray(inp["L1_attn_w_qkv"])
    w["bqkv2"] = np.ascontiguousarray(inp["L1_attn_b_qkv"].reshape(KH, 128).T)
    w["wgate"] = np.ascontiguousarray(inp["L1_attn_w_gate"])
    w["bgate2"] = np.ascontiguousarray(inp["L1_attn_b_gate"].reshape(KD, 128).T)
    w["wout1"] = np.ascontiguousarray(inp["L1_attn_w_out"])
    w["bout2"] = np.ascontiguousarray(inp["L1_attn_b_out"].reshape(KD, 128).T)
    pb = inp["L1_attn_pos_bias"]
    pb88 = np.zeros((88, 8), np.float32)
    for s in range(44):
        for hp in range(8):
            for hs in range(2):
                pb88[s * 2 + hs, hp] = pb[s, 2 * hp + hs]
    w["pb88"] = pb88
    Emat = np.zeros((NOFF, 128, 88), np.float32)
    for oi, off in enumerate(DOFFS):
        for s in range(44):
            if OFFSETS[s] == off:
                for hs in range(2):
                    Emat[oi, hs * 64:(hs + 1) * 64, s * 2 + hs] = 1.0
    w["Emat"] = Emat.reshape(NOFF * 128, 88)
    G = np.zeros((88, 44), np.float32)
    for s in range(44):
        if OFFSETS[s] < T:
            oi = DOFFS.index(OFFSETS[s])
            for hs in range(2):
                G[s * 2 + hs, oi * 2 + hs] = 1.0
    w["Gmat"] = G
    Smat = np.zeros((NOFF, 44, 128), np.float32)
    for oi in range(NOFF):
        Smat[oi, oi * 2 + 0, 0:64] = 1.0
        Smat[oi, oi * 2 + 1, 64:128] = 1.0
    w["Smat"] = np.ascontiguousarray(Smat.transpose(1, 0, 2).reshape(44, NOFF * 128))
    Eh = np.zeros((88, 2), np.float32)
    for s in range(44):
        for hs in range(2):
            Eh[s * 2 + hs, hs] = 1.0
    w["Ehmat"] = Eh
    Ebc = np.zeros((2, 44), np.float32)
    for oi in range(NOFF):
        Ebc[0, oi * 2 + 0] = 1.0
        Ebc[1, oi * 2 + 1] = 1.0
    w["Ebc"] = Ebc
    w["wgT"] = np.ascontiguousarray(inp["L1_ffn_w_gate"].T)
    w["wuT"] = np.ascontiguousarray(inp["L1_ffn_w_up"].T)
    w["wdT"] = np.ascontiguousarray(inp["L1_ffn_w_down"].T)
    return w


WSPEC = {
    "x": ((D, BT), F32), "w_in": ((D, 2 * DI), F32), "dwdiag": ((4 * KDI * 128, 128), F32),
    "pw": ((DI, DI), F32), "cb2": ((128, KDI), F32), "wx": ((DI, 80), F32),
    "wdt": ((64, DI), F32), "bdt2": ((128, KDI), F32), "alog": ((DI, DS), F32),
    "dvec": ((128, KDI), F32), "wout": ((DI, D), F32),
    "wrT": ((D, D), F32), "wkT": ((D, D), F32), "wvT": ((D, D), F32),
    "mur": ((128, KD), F32), "muk": ((128, KD), F32), "esel": ((16, 16 * 128), F32),
    "wqkv": ((D, 3 * D), F32), "bqkv2": ((128, KH), F32), "wgate": ((D, D), F32),
    "bgate2": ((128, KD), F32), "wout1": ((D, D), F32), "bout2": ((128, KD), F32),
    "pb88": ((88, 8), F32), "Emat": ((NOFF * 128, 88), F32), "Gmat": ((88, 44), F32),
    "Smat": ((44, NOFF * 128), F32), "Ehmat": ((88, 2), F32), "Ebc": ((2, 44), F32),
    "wgT": ((D, HID), I32), "wuT": ((D, HID), I32), "wdT": ((HID, D), I32),
}
SCRATCH = {
    "xn": ((D, BT), BF16), "ffn0": ((D, BT), F32), "sz": ((DI, BT), BF16),
    "xc": ((DI, BT), BF16), "dlr": ((64, BT), BF16), "bc": ((16, BT), BF16),
    "ywz": ((DI, BT), BF16), "x1": ((D, BT), F32), "xn1": ((D, BT), BF16),
    "at1": ((D, BT), BF16), "hmid": ((HID, BT), BF16), "outx": ((D, BT), F32),
}


def wload_bf16(ph, e, src_ap, krows, cols, name, stg=None, stg_e=None, dst=None):
    nc = ph.nc
    if stg is None:
        stg = (stg_e or e)(nc.sbuf_tensor(f"{name}_stg", [128, 2, min(cols, 2048)], F32))
    if dst is None:
        dst = e(nc.sbuf_tensor(f"{name}_bf", [128, krows, cols], BF16))
    cap = stg.shape[2]
    src = src_ap.rearrange("(a p) m -> p a m", p=128)
    i = 0
    for k in range(krows):
        for c0 in range(0, cols, cap):
            cw = min(cap, cols - c0)
            ph.load(stg[:, i % 2, 0:cw], src[:, k, c0:c0 + cw], wait=[("wcv", ph.c("wcv"))])
            ph.op("vector", lambda eng, k=k, c0=c0, cw=cw, i=i: eng.tensor_copy(
                dst[:, k, c0:c0 + cw], stg[:, i % 2, 0:cw]),
                inc=[("wcv", 1)], wait=[("ld", ph.c("ld"))])
            i += 1
    return dst


def wload_bf16_i32(ph, e, src_ap, krows, cols, name, stg_e=None):
    nc = ph.nc
    stg = (stg_e or e)(nc.sbuf_tensor(f"{name}_stg", [128, 2, cols], I32))
    dst = e(nc.sbuf_tensor(f"{name}_bf", [128, krows, cols], BF16))
    src = src_ap.rearrange("(a p) m -> p a m", p=128)
    for k in range(krows):
        ph.load(stg[:, k % 2, :], src[:, k, :], wait=[("wcv", ph.c("wcv"))])
        ph.op("vector", lambda eng, k=k: eng.tensor_copy(dst[:, k, :], stg[:, k % 2, :]),
              inc=[("wcv", 1)], wait=[("ld", ph.c("ld"))])
    return dst


def wload_small_bf16(ph, e, src_ap, rows, cols, name):
    nc = ph.nc
    stg = e(nc.sbuf_tensor(f"{name}_stg", [rows, cols], F32))
    dst = e(nc.sbuf_tensor(f"{name}_bf", [rows, cols], BF16))
    ph.load(stg[:], src_ap)
    ph.op("vector", lambda eng: eng.tensor_copy(dst[:], stg[:]),
          inc=[("wcv", 1)], wait=[("ld", ph.c("ld"))])
    return dst



def build_ln(ph, e, dram, src, dst):
    nc = ph.nc
    xin, xout = dram(src), dram(dst)
    stg = e(nc.sbuf_tensor("ln_stg", [128, 2, T], F32))
    xb = e(nc.sbuf_tensor("ln_xb", [128, KD, T], BF16))
    xq = e(nc.sbuf_tensor("ln_xq", [128, KD, T], BF16))
    m_sb = e(nc.sbuf_tensor("ln_m", [128, T], F32))
    m2 = e(nc.sbuf_tensor("ln_m2", [128, T], F32))
    v_sb = e(nc.sbuf_tensor("ln_v", [128, T], F32))
    rstd = e(nc.sbuf_tensor("ln_rstd", [128, T], F32))
    tmp = e(nc.sbuf_tensor("ln_tmp", [128, T], F32))
    xn_sb = e(nc.sbuf_tensor("ln_xn", [128, KD, T], BF16))
    onesm = e(nc.sbuf_tensor("ln_ones", [128, 128], BF16))
    ps_m = e(nc.psum_tensor("ln_psm", [128, T], F32))
    ps_v = e(nc.psum_tensor("ln_psv", [128, T], F32))

    ph.op("vector", lambda eng: eng.memset(onesm[:], 1.0 / D), inc=[("vec", 1)])
    ph.close("vec")
    for b in range(NB):
        for kt in range(KD):
            lsem = "ld" if kt % 2 == 0 else "ldb"
            ph.load(stg[:, kt % 2, :], xin[kt * 128:(kt + 1) * 128, b * T:(b + 1) * T],
                    wait=[("cv", ph.c("cv") - 1), ("cw", ph.c("cw") - 1)], sem=lsem)
            ph.op("vector", lambda eng, kt=kt: eng.tensor_copy(xb[:, kt, :], stg[:, kt % 2, :]),
                  inc=[("cv", 1)], wait=[(lsem, ph.c(lsem))])
            ph.op("scalar", lambda eng, kt=kt: eng.activation(xq[:, kt, :], stg[:, kt % 2, :], AF.Square),
                  inc=[("cw", 1)], wait=[(lsem, ph.c(lsem))])
        ph.close("cv", "cw")
        for c in range(NCH):
            for kt in range(KD):
                ph.op("tensor", lambda eng, c=c, kt=kt: eng.matmul(
                    ps_m[:, c * CH:(c + 1) * CH], onesm[:], xb[:, kt, c * CH:(c + 1) * CH],
                    start=(kt == 0), stop=(kt == KD - 1)))
            for kt in range(KD):
                ph.op("tensor", lambda eng, c=c, kt=kt: eng.matmul(
                    ps_v[:, c * CH:(c + 1) * CH], onesm[:], xq[:, kt, c * CH:(c + 1) * CH],
                    start=(kt == 0), stop=(kt == KD - 1)),
                    inc=[("mm", 1)] if (c == NCH - 1 and kt == KD - 1) else ())
        ph.close("mm")
        ph.op("scalar", lambda eng: eng.copy(m_sb[:], ps_m[:]), inc=[("sp", 1)])
        ph.close("sp")
        ph.op("vector", lambda eng: eng.tensor_mul(m2[:], m_sb[:], m_sb[:]))
        ph.op("vector", lambda eng: eng.tensor_tensor(v_sb[:], ps_v[:], m2[:], ALU.subtract),
              inc=[("vec", 1)])
        ph.close("vec")
        ph.op("scalar", lambda eng: eng.activation(tmp[:], v_sb[:], AF.Ln, bias=np.float32(EPS)))
        ph.op("scalar", lambda eng: eng.activation(rstd[:], tmp[:], AF.Exp, scale=np.float32(-0.5)),
              inc=[("sp", 1)])
        ph.close("sp")
        for kt in range(KD):
            nsem = "st" if kt % 2 == 0 else "stb"
            ph.op("vector", lambda eng, kt=kt: eng.tensor_tensor(tmp[:], xb[:, kt, :], m_sb[:], ALU.subtract))
            ph.op("vector", lambda eng, kt=kt: eng.tensor_mul(xn_sb[:, kt, :], tmp[:], rstd[:]),
                  inc=[("vec", 1)], wait=[(nsem, ph.c(nsem))])
            ph.store(xout[kt * 128:(kt + 1) * 128, b * T:(b + 1) * T], xn_sb[:, kt, :],
                     wait=[("vec", ph.c("vec"))], sem=nsem)
        ph.close("vec")


def build_l0ffn(ph, e, dram):
    nc = ph.nc
    xn, out = dram("xn"), dram("ffn0")
    wstg = e(nc.sbuf_tensor("ffn_wstg", [128, 2, D], F32))
    wr = wload_bf16(ph, e, dram("wrT"), KD, D, "wr", stg=wstg)
    wk = wload_bf16(ph, e, dram("wkT"), KD, D, "wk", stg=wstg)
    wv = wload_bf16(ph, e, dram("wvT"), KD, D, "wv", stg=wstg)
    mur = e(nc.sbuf_tensor("mur_sb", [128, KD], F32))
    muk = e(nc.sbuf_tensor("muk_sb", [128, KD], F32))
    mur1 = e(nc.sbuf_tensor("mur1_sb", [128, KD], F32))
    muk1 = e(nc.sbuf_tensor("muk1_sb", [128, KD], F32))
    ph.load(mur[:], dram("mur"))
    ph.load(muk[:], dram("muk"))
    ph.op("vector", lambda eng: eng.tensor_scalar(mur1[:], mur[:], -1.0, 1.0, ALU.mult, ALU.add),
          wait=[("ld", ph.c("ld"))])
    ph.op("vector", lambda eng: eng.tensor_scalar(muk1[:], muk[:], -1.0, 1.0, ALU.mult, ALU.add),
          inc=[("vec", 1)])

    xnp = e(nc.sbuf_tensor("ffn_xnp", [128, KD, T + 1], BF16))
    xr = e(nc.sbuf_tensor("ffn_xr", [128, KD, T], BF16))
    xk = xr
    t1 = e(nc.sbuf_tensor("ffn_t1", [128, T], BF16))
    r_sb = e(nc.sbuf_tensor("ffn_r", [128, KD, T], BF16))
    k_sb = e(nc.sbuf_tensor("ffn_k", [128, KD, T], BF16))
    oo = e(nc.sbuf_tensor("ffn_oo", [128, 2, T], F32))
    ps = e(nc.psum_tensor("ffn_ps", [128, 2, T], F32))

    ph.op("vector", lambda eng: eng.memset(xnp[:, :, 0:1], 0.0), inc=[("vec", 1)])
    ph.close("vec", "wcv")

    def mm_block(w, rhs, m, extra_wait):
        for c in range(NCH):
            for k in range(KD):
                ph.op("tensor", lambda eng, m=m, c=c, k=k, w=w, rhs=rhs: eng.matmul(
                    ps[:, m % 2, c * CH:(c + 1) * CH], w[:, k, m * 128:(m + 1) * 128],
                    rhs[:, k, c * CH:(c + 1) * CH], start=(k == 0), stop=(k == KD - 1)),
                    inc=[("mm", 1)] if (c == NCH - 1 and k == KD - 1) else (),
                    wait=extra_wait if (c == 0 and k == 0) else ())

    for b in range(NB):
        ph.load(xnp[:, :, 1:T + 1], xn[:, b * T:(b + 1) * T].rearrange("(k p) t -> p k t", p=128))
        ph.close("ld")
        for kt in range(KD):
            ph.op("vector", lambda eng, kt=kt: eng.tensor_scalar(
                t1[:], xnp[:, kt, 0:T], mur1[:, kt:kt + 1], None, ALU.mult))
            ph.op("vector", lambda eng, kt=kt: eng.scalar_tensor_tensor(
                xr[:, kt, :], xnp[:, kt, 1:T + 1], mur[:, kt:kt + 1], t1[:], ALU.mult, ALU.add),
                inc=[("vec", 1)])
        ph.close("vec")
        for m in range(KD):
            mm_block(wr, xr, m, [("sp", ph.c("sp") - 1)])
            ph.op("scalar", lambda eng, m=m: eng.activation(r_sb[:, m, :], ps[:, m % 2, :], AF.Sigmoid),
                  inc=[("sp", 1)], wait=[("mm", ph.c("mm"))])
        ph.close("mm", "sp")
        for kt in range(KD):
            ph.op("vector", lambda eng, kt=kt: eng.tensor_scalar(
                t1[:], xnp[:, kt, 0:T], muk1[:, kt:kt + 1], None, ALU.mult))
            ph.op("vector", lambda eng, kt=kt: eng.scalar_tensor_tensor(
                xk[:, kt, :], xnp[:, kt, 1:T + 1], muk[:, kt:kt + 1], t1[:], ALU.mult, ALU.add),
                inc=[("vec", 1)])
        ph.close("vec")
        for m in range(KD):
            mm_block(wk, xk, m, [("sp", ph.c("sp") - 1)])
            ph.op("scalar", lambda eng, m=m: eng.activation(k_sb[:, m, :], ps[:, m % 2, :], AF.Relu),
                  inc=[("sp", 1)], wait=[("mm", ph.c("mm"))])
        ph.close("sp")
        for m in range(KD):
            ph.op("vector", lambda eng, m=m: eng.tensor_mul(k_sb[:, m, :], k_sb[:, m, :], k_sb[:, m, :]),
                  inc=[("vec", 1)])
        ph.close("vec")
        for m in range(KD):
            mm_block(wv, k_sb, m, [("vec", ph.c("vec"))])
            osem = "st" if m % 2 == 0 else "stb"
            ph.op("vector", lambda eng, m=m: eng.tensor_tensor(oo[:, m % 2, :], r_sb[:, m, :], ps[:, m % 2, :], ALU.mult),
                  inc=[("vec", 1)], wait=[("mm", ph.c("mm")), (osem, ph.c(osem))])
            ph.store(out[m * 128:(m + 1) * 128, b * T:(b + 1) * T], oo[:, m % 2, :],
                     wait=[("vec", ph.c("vec"))], sem=osem)
        ph.close("st", "vec")


def build_mamba_pre(ph, e, dram):
    nc = ph.nc
    xn = dram("xn")
    sz_o, xc_o, dlr_o, bc_o = dram("sz"), dram("xc"), dram("dlr"), dram("bc")
    win = e(nc.sbuf_tensor("win_bf", [128, KD, 2 * DI], BF16))
    pw = e(nc.sbuf_tensor("pw_bf", [128, KDI, DI], BF16))
    dwd = e(nc.sbuf_tensor("dwd_bf", [128, 4 * KDI, 128], BF16))
    wx = e(nc.sbuf_tensor("wx_bf", [128, KDI, 80], BF16))
    cb2 = e(nc.sbuf_tensor("cb2_sb", [128, KDI], F32))
    with nc.sbuf_tensor("mp_wstg", [128, 2, 2 * DI], F32) as wstg:
        wload_bf16(ph, e, dram("w_in"), KD, 2 * DI, "win", stg=wstg, dst=win)
        wload_bf16(ph, e, dram("pw"), KDI, DI, "pw", stg=wstg, dst=pw)
        wload_bf16(ph, e, dram("dwdiag"), 4 * KDI, 128, "dwd", stg=wstg, dst=dwd)
        wload_bf16(ph, e, dram("wx"), KDI, 80, "wx", stg=wstg, dst=wx)
        ph.load(cb2[:], dram("cb2"))
        ph.close("wcv", "ld")

    xnc = e(nc.sbuf_tensor("mp_xnc", [128, KD, CH], BF16))
    xin_c = e(nc.sbuf_tensor("mp_xin", [128, KDI, 3 + CH], BF16))
    cvo = e(nc.sbuf_tensor("mp_cvo", [128, KDI, CH], BF16))
    szc = e(nc.sbuf_tensor("mp_szc", [128, 2, CH], BF16))
    xcc = e(nc.sbuf_tensor("mp_xcc", [128, KDI, CH], BF16))
    dlrc = e(nc.sbuf_tensor("mp_dlr", [64, CH], BF16))
    bbc = e(nc.sbuf_tensor("mp_b", [8, CH], BF16))
    ccc = e(nc.sbuf_tensor("mp_c", [8, CH], BF16))
    ps = e(nc.psum_tensor("mp_ps", [128, 4, CH], F32))
    ps2 = e(nc.psum_tensor("mp_ps2", [128, 4, CH], F32))

    for b in range(NB):
        ph.op("vector", lambda eng: eng.memset(xin_c[:, :, 0:3], 0.0), inc=[("vec", 1)])
        ph.close("vec")
        for c in range(NCH):
            t0 = b * T + c * CH
            gi = b * NCH + c
            if gi == 0:
                ph.load(xnc[:, :, :], xn[:, t0:t0 + CH].rearrange("(k p) t -> p k t", p=128))
            ph.close("ld")
            for m in range(32):
                for k in range(KD):
                    ph.op("tensor", lambda eng, m=m, k=k: eng.matmul(
                        ps[:, m % 4, :], win[:, k, m * 128:(m + 1) * 128], xnc[:, k, :],
                        start=(k == 0), stop=(k == KD - 1)),
                        inc=[("mm", 1)] if k == KD - 1 else (),
                        wait=[("sp", ph.c("sp") - (3 if m > 0 else 0))] if k == 0 else ())
                if m < 16:
                    ph.op("scalar", lambda eng, m=m: eng.copy(xin_c[:, m, 3:3 + CH], ps[:, m % 4, :]),
                          inc=[("sp", 1)],
                          wait=[("mm", ph.c("mm"))] + ([("vec", ph.c("vec"))] if m == 0 else []))
                else:
                    ph.op("scalar", lambda eng, m=m: eng.activation(
                        szc[:, m % 2, :], ps[:, m % 4, :], AF.Silu),
                        inc=[("sp", 1)], wait=[("mm", ph.c("mm")), ("st", ph.c("st"))])
                    if m % 2 == 1:
                        ph.store(sz_o[(m - 17) * 128:(m - 15) * 128, t0:t0 + CH].rearrange(
                            "(k p) t -> p k t", p=128), szc[:, :, :],
                            wait=[("sp", ph.c("sp"))])
            if gi + 1 < NB * NCH:
                t1 = b * T + (c + 1) * CH if c + 1 < NCH else (b + 1) * T
                ph.load(xnc[:, :, :], xn[:, t1:t1 + CH].rearrange("(k p) t -> p k t", p=128),
                        wait=[("mm", ph.c("mm"))])
            ph.close("sp")
            for kt in range(KDI):
                for j in range(4):
                    ph.op("tensor", lambda eng, kt=kt, j=j: eng.matmul(
                        ps2[:, kt % 4, :], dwd[:, j * 16 + kt, :], xin_c[:, kt, j:j + CH],
                        start=(j == 0), stop=(j == 3)),
                        inc=[("mm", 1)] if j == 3 else (),
                        wait=[("sp", ph.c("sp") - (3 if kt > 0 else 0))] if j == 0 else ())
                ph.op("scalar", lambda eng, kt=kt: eng.copy(cvo[:, kt, :], ps2[:, kt % 4, :]),
                      inc=[("sp", 1)], wait=[("mm", ph.c("mm"))])
            ph.close("sp")
            for m in range(KDI):
                for k in range(KDI):
                    ph.op("tensor", lambda eng, m=m, k=k: eng.matmul(
                        ps[:, m % 4, :], pw[:, k, m * 128:(m + 1) * 128], cvo[:, k, :],
                        start=(k == 0), stop=(k == KDI - 1)),
                        inc=[("mm", 1)] if k == KDI - 1 else (),
                        wait=[("sp", ph.c("sp") - 3)] if k == 0 else ())
                ph.op("scalar", lambda eng, m=m: eng.activation(
                    xcc[:, m, :], ps[:, m % 4, :], AF.Silu, bias=cb2[:, m:m + 1], scale=np.float32(1.0)),
                    inc=[("sp", 1)],
                    wait=[("mm", ph.c("mm"))] + ([("st", ph.c("st"))] if m == 0 else []))
            ph.close("sp")
            ph.store(xc_o[:, t0:t0 + CH].rearrange("(k p) t -> p k t", p=128), xcc[:, :, :])
            for k in range(KDI):
                ph.op("tensor", lambda eng, k=k: eng.matmul(
                    ps2[0:64, 0, :], wx[:, k, 0:64], xcc[:, k, :],
                    start=(k == 0), stop=(k == KDI - 1)),
                    wait=[("sp", ph.c("sp"))] if k == 0 else ())
            for k in range(KDI):
                ph.op("tensor", lambda eng, k=k: eng.matmul(
                    ps2[0:8, 1, :], wx[:, k, 64:72], xcc[:, k, :],
                    start=(k == 0), stop=(k == KDI - 1)))
            for k in range(KDI):
                ph.op("tensor", lambda eng, k=k: eng.matmul(
                    ps[0:8, 0, :], wx[:, k, 72:80], xcc[:, k, :],
                    start=(k == 0), stop=(k == KDI - 1)),
                    inc=[("mm", 1)] if k == KDI - 1 else ())
            ph.close("mm")
            ph.op("scalar", lambda eng: eng.copy(dlrc[:], ps2[0:64, 0, :]), wait=[("st", ph.c("st"))])
            ph.op("scalar", lambda eng: eng.copy(bbc[:], ps2[0:8, 1, :]))
            ph.op("scalar", lambda eng: eng.copy(ccc[:], ps[0:8, 0, :]), inc=[("sp", 1)])
            ph.close("sp")
            ph.store(dlr_o[:, t0:t0 + CH], dlrc[:])
            ph.store(bc_o[0:8, t0:t0 + CH], bbc[:])
            ph.store(bc_o[8:16, t0:t0 + CH], ccc[:])
            ph.op("vector", lambda eng: eng.tensor_copy(xin_c[:, :, 0:3], xin_c[:, :, CH:CH + 3]),
                  inc=[("vec", 1)], wait=[("mm", ph.c("mm"))])
            ph.close("vec")


def build_scan(ph, e, dram):
    nc = ph.nc
    xc_i, dlr_i, bc_i, sz_i = dram("xc"), dram("dlr"), dram("bc"), dram("sz")
    ywz_o = dram("ywz")
    wdt = wload_small_bf16(ph, e, dram("wdt"), 64, DI, "wdt")
    eself = e(nc.sbuf_tensor("esel_f", [16, 16, 128], F32))
    eselb = e(nc.sbuf_tensor("esel_b", [16, 16, 128], BF16))
    ph.load(eself[:], dram("esel").rearrange("i (k p) -> k i p", k=16))
    ph.op("vector", lambda eng: eng.tensor_copy(eselb[:], eself[:]),
          inc=[("wcv", 1)], wait=[("ld", ph.c("ld"))])
    bdt2 = e(nc.sbuf_tensor("bdt2_sb", [128, KDI], F32))
    dvec = e(nc.sbuf_tensor("dvec_sb", [128, KDI], F32))
    alog = e(nc.sbuf_tensor("alog_sb", [128, KDI, DS], F32))
    a_mat = e(nc.sbuf_tensor("a_sb", [128, KDI, DS], F32))
    ph.load(bdt2[:], dram("bdt2"))
    ph.load(dvec[:], dram("dvec"))
    ph.load(alog[:], dram("alog").rearrange("(k p) n -> p k n", p=128))
    ph.op("scalar", lambda eng: eng.activation(a_mat[:], alog[:], AF.Exp),
          inc=[("sp", 1)], wait=[("ld", ph.c("ld"))])
    ph.op("vector", lambda eng: eng.tensor_scalar(a_mat[:], a_mat[:], -1.0, None, ALU.mult),
          inc=[("vec", 1)], wait=[("sp", ph.c("sp"))])
    ph.close("vec", "wcv")

    dlr_b = e(nc.sbuf_tensor("sc_dlr", [64, T], BF16))
    bcb = e(nc.sbuf_tensor("sc_bcb", [16, T], BF16))
    bb = e(nc.sbuf_tensor("sc_bb", [128, 16, T], BF16))
    xct = e(nc.sbuf_tensor("sc_xct", [128, 2, T], BF16))
    szt = e(nc.sbuf_tensor("sc_szt", [128, 2, T], BF16))
    delta = e(nc.sbuf_tensor("sc_delta", [128, 2, T], BF16))
    te_b = e(nc.sbuf_tensor("sc_te", [128, T], F32))
    du = e(nc.sbuf_tensor("sc_du", [128, T], BF16))
    a_b = e(nc.sbuf_tensor("sc_a", [128, 4, T], BF16))
    u_b = e(nc.sbuf_tensor("sc_u", [128, T], BF16))
    h_b = u_b
    p_b = e(nc.sbuf_tensor("sc_p", [128, 2, T], BF16))
    y_b = e(nc.sbuf_tensor("sc_y", [128, T], BF16))
    yw = e(nc.sbuf_tensor("sc_yw", [128, 2, T], BF16))
    ps_d = e(nc.psum_tensor("sc_psd", [128, T], F32))
    ps_b = e(nc.psum_tensor("sc_psb", [128, T], F32))

    def sched_chain(dt, sp_exact=False):
        for c in range(NCH):
            if c == 0:
                w0 = [("sp", ph.c("sp") if sp_exact else ph.c("sp") - 8)]
            else:
                w0 = ()
            ph.op("tensor", lambda eng, dt=dt, c=c: eng.matmul(
                ps_d[:, c * CH:(c + 1) * CH], wdt[:, dt * 128:(dt + 1) * 128],
                dlr_b[:, c * CH:(c + 1) * CH], start=True, stop=True),
                inc=[("mm", 1)] if c == NCH - 1 else (),
                wait=w0)
        ph.op("scalar", lambda eng, dt=dt: eng.activation(
            te_b[:], ps_d[:], AF.Exp, bias=bdt2[:, dt:dt + 1], scale=np.float32(1.0)),
            wait=[("mm", ph.c("mm")), ("vec", ph.c("vec") - 10)])
        ph.op("scalar", lambda eng, dt=dt: eng.activation(
            delta[:, dt % 2, :], te_b[:], AF.Ln, bias=np.float32(1.0)),
            inc=[("sp", 1)])

    def agen(dt, n, tb):
        ph.op("scalar", lambda eng, dt=dt, n=n: eng.activation(
            a_b[:, n % 4, :], delta[:, dt % 2, :], AF.Exp, scale=a_mat[:, dt, n:n + 1]),
            inc=[("sp", 1)], wait=[("tts", tb + n - 3)] if tb + n - 3 > 0 else ())

    BORD = [0, 8, 1, 9, 2, 10, 3, 11, 4, 12, 5, 13, 6, 14, 7, 15]
    for b in range(NB):
        ph.load(dlr_b[:], dlr_i[:, b * T:(b + 1) * T], wait=[("vec", ph.c("vec"))])
        ph.load(bcb[:], bc_i[:, b * T:(b + 1) * T])
        ph.load(xct[:, 0, :], xc_i[0:128, b * T:(b + 1) * T])
        ph.load(szt[:, 0, :], sz_i[0:128, b * T:(b + 1) * T])
        ph.close("ld")
        sched_chain(0)
        sLn = ph.c("sp")
        tb0 = ph.c("tts")
        sAg = {}
        sEv = {}
        for n in range(4):
            agen(0, n, tb0)
            sAg[n] = ph.c("sp")
        for j, i in enumerate(BORD):
            if j >= 8 and j % 2 == 0:
                n = 4 + (j - 8) // 2
                agen(0, n, tb0)
                sAg[n] = ph.c("sp")
            pst = ps_b if j % 2 == 0 else ps_d
            for c in range(NCH):
                ph.op("tensor", lambda eng, i=i, c=c, pst=pst: eng.matmul(
                    pst[:, c * CH:(c + 1) * CH], eselb[:, i, :],
                    bcb[:, c * CH:(c + 1) * CH], start=True, stop=True),
                    inc=[("mm", 1)] if c == NCH - 1 else (),
                    wait=([("sp", ph.c("sp"))] if j == 0 else [("sp", ph.c("sp") - 1)]) if c == 0 else ())
            ph.op("scalar", lambda eng, i=i, pst=pst: eng.copy(bb[:, i, :], pst[:]),
                  inc=[("sp", 1)], wait=[("mm", ph.c("mm"))])
            sEv[i] = ph.c("sp")
        for dt in range(KDI):
            if dt + 1 < KDI:
                lsem = "ld" if (dt + 1) % 2 == 0 else "ldb"
                ph.load(xct[:, (dt + 1) % 2, :], xc_i[(dt + 1) * 128:(dt + 2) * 128, b * T:(b + 1) * T],
                        wait=[("vec", ph.c("vec") - 2)] if dt >= 1 else (), sem=lsem)
                ph.load(szt[:, (dt + 1) % 2, :], sz_i[(dt + 1) * 128:(dt + 2) * 128, b * T:(b + 1) * T],
                        sem=lsem)
            dsem = "ld" if dt % 2 == 0 else "ldb"
            ph.op("vector", lambda eng, dt=dt: eng.tensor_tensor(du[:], delta[:, dt % 2, :], xct[:, dt % 2, :], ALU.mult),
                  wait=[("sp", sLn if dt == 0 else ph.c("sp")), (dsem, ph.c(dsem))])
            ph.op("vector", lambda eng, dt=dt: eng.tensor_scalar(
                y_b[:], xct[:, dt % 2, :], dvec[:, dt:dt + 1], None, ALU.mult),
                inc=[("vec", 1)])
            tts_base = ph.c("tts")
            for n in range(DS):
                if dt > 0:
                    agen(dt, n, tts_base)
                ph.op("vector", lambda eng, n=n: eng.tensor_tensor(u_b[:], du[:], bb[:, n, :], ALU.mult),
                      wait=[("sp", sEv[n])] if dt == 0 else ())
                ph.op("vector", lambda eng, n=n: eng.tensor_tensor_scan(
                    u_b[:], a_b[:, n % 4, :], u_b[:], 0.0, ALU.mult, ALU.add),
                    inc=[("tts", 1)], wait=[("sp", sAg[n] if dt == 0 else ph.c("sp"))])
                ph.op("vector", lambda eng, n=n: eng.tensor_tensor(
                    p_b[:, n % 2, :], h_b[:], bb[:, 8 + n, :], ALU.mult),
                    wait=[("sp", sEv[8 + n])] if dt == 0 else ())
                ph.op("vector", lambda eng, n=n: eng.tensor_tensor(
                    y_b[:], y_b[:], p_b[:, n % 2, :], ALU.add), inc=[("vec", 1)])
            ssem = "st" if dt % 2 == 0 else "stb"
            ph.op("vector", lambda eng, dt=dt: eng.tensor_tensor(yw[:, dt % 2, :], y_b[:], szt[:, dt % 2, :], ALU.mult),
                  inc=[("vec", 1)], wait=[(ssem, ph.c(ssem))])
            ph.store(ywz_o[dt * 128:(dt + 1) * 128, b * T:(b + 1) * T], yw[:, dt % 2, :],
                     wait=[("vec", ph.c("vec"))], sem=ssem)
            if dt + 1 < KDI:
                sched_chain(dt + 1, sp_exact=(dt == 0))
        ph.close("vec", "st", "stb", "sp", "tts")


def build_l0out(ph, e, dram):
    nc = ph.nc
    ywz_i, x_i, f_i, x1_o = dram("ywz"), dram("x"), dram("ffn0"), dram("x1")
    wout = e(nc.sbuf_tensor("wo_bf", [128, KDI, D], BF16))
    with nc.sbuf_tensor("wo_stg", [128, 2, D], F32) as wstg:
        wload_bf16(ph, e, dram("wout"), KDI, D, "wo", stg=wstg, dst=wout)
        ph.close("wcv", "ld")
    yw_sb = e(nc.sbuf_tensor("lo_yw", [128, 2 * (KDI - 1) + 1, T], BF16))
    xs = e(nc.sbuf_tensor("lo_xs", [128, 2, T], F32))
    fs = e(nc.sbuf_tensor("lo_fs", [128, 2, T], F32))
    o_sb = e(nc.sbuf_tensor("lo_o", [128, 2, T], F32))
    ps = e(nc.psum_tensor("lo_ps", [128, 2, T], F32))
    ph.close("wcv", "ld")
    ph.load(yw_sb[:, 0:15, :], ywz_i[0:15 * 128, 0:T].rearrange("(k p) t -> p k t", p=128))
    for b in range(NB):
        ph.load(xs[:, 0, :], x_i[0:128, b * T:(b + 1) * T],
                wait=[("vec", ph.c("vec"))], sem="lde")
        ph.load(fs[:, 0, :], f_i[0:128, b * T:(b + 1) * T], sem="lde")
        ph.load(yw_sb[:, 30, :], ywz_i[15 * 128:16 * 128, b * T:(b + 1) * T],
                wait=[("mm", ph.c("mm"))])
        for m in range(KD):
            if m + 1 < KD:
                lsem = "lde" if (m + 1) % 2 == 0 else "ldf"
                ph.load(xs[:, (m + 1) % 2, :], x_i[(m + 1) * 128:(m + 2) * 128, b * T:(b + 1) * T],
                        wait=[("vec", ph.c("vec"))], sem=lsem)
                ph.load(fs[:, (m + 1) % 2, :], f_i[(m + 1) * 128:(m + 2) * 128, b * T:(b + 1) * T],
                        sem=lsem)
            for c in range(NCH):
                for k in range(KDI):
                    if m == 0 and c == 0 and k == 0:
                        w0 = [("ld", ph.c("ld"))] + ([("vec", ph.c("vec") - 2)] if ph.c("vec") >= 2 else [])
                    elif c == 0 and k == 0:
                        w0 = [("vec", ph.c("vec") - 2)] if ph.c("vec") >= 2 else ()
                    else:
                        w0 = ()
                    sk = 30 if k == KDI - 1 else (b % 2) * 15 + k
                    ph.op("tensor", lambda eng, m=m, c=c, k=k, sk=sk: eng.matmul(
                        ps[:, m % 2, c * CH:(c + 1) * CH], wout[:, k, m * 128:(m + 1) * 128],
                        yw_sb[:, sk, c * CH:(c + 1) * CH], start=(k == 0), stop=(k == KDI - 1)),
                        inc=[("mm", 1)] if (c == NCH - 1 and k == KDI - 1) else (),
                        wait=w0)
            if m == 0 and b + 1 < NB:
                ph.load(yw_sb[:, ((b + 1) % 2) * 15:((b + 1) % 2) * 15 + 15, :],
                        ywz_i[0:15 * 128, (b + 1) * T:(b + 2) * T].rearrange("(k p) t -> p k t", p=128),
                        wait=[("mm", ph.c("mm"))])
            dsem = "lde" if m % 2 == 0 else "ldf"
            ssem = "st" if m % 2 == 0 else "stb"
            ph.op("vector", lambda eng, m=m: eng.tensor_tensor(o_sb[:, m % 2, :], ps[:, m % 2, :], xs[:, m % 2, :], ALU.add),
                  wait=[("mm", ph.c("mm")), (dsem, ph.c(dsem)), (ssem, ph.c(ssem))])
            ph.op("vector", lambda eng, m=m: eng.tensor_tensor(o_sb[:, m % 2, :], o_sb[:, m % 2, :], fs[:, m % 2, :], ALU.add),
                  inc=[("vec", 2)])
            ph.store(x1_o[m * 128:(m + 1) * 128, b * T:(b + 1) * T], o_sb[:, m % 2, :],
                     wait=[("vec", ph.c("vec"))], sem=ssem)
        ph.close("vec", "st", "stb")


def build_l1attn(ph, e, dram):
    nc = ph.nc
    xn1_i, at1_o = dram("xn1"), dram("at1")
    wqkv = e(nc.sbuf_tensor("wqkv_bf", [128, KD, 3 * D], BF16))
    wgate = e(nc.sbuf_tensor("wg1_bf", [128, KD, D], BF16))
    wout1 = e(nc.sbuf_tensor("wo1_bf", [128, KD, D], BF16))
    em = e(nc.sbuf_tensor("em_bf", [128, NOFF, 88], BF16))
    smt = e(nc.sbuf_tensor("smt_b", [44, NOFF, 128], BF16))
    gm = e(nc.sbuf_tensor("gm_bf", [88, 44], BF16))
    ehm = e(nc.sbuf_tensor("ehm_bf", [88, 2], BF16))
    ebc = e(nc.sbuf_tensor("ebc_sb", [2, 44], F32))
    pb = e(nc.sbuf_tensor("pb_sb", [88, 8], F32))
    bqkv = e(nc.sbuf_tensor("bqkv_sb", [128, KH], F32))
    bgate = e(nc.sbuf_tensor("bgate_sb", [128, KD], F32))
    bout = e(nc.sbuf_tensor("bout_sb", [128, KD], F32))
    xn1 = e(nc.sbuf_tensor("la_xn1", [128, KD, T], BF16))
    q_sb = e(nc.sbuf_tensor("la_q", [128, T], BF16))
    kpad = e(nc.sbuf_tensor("la_kp", [128, MAXOFF + T], BF16))
    vpad = e(nc.sbuf_tensor("la_vp", [128, MAXOFF + T], BF16))
    prod = e(nc.sbuf_tensor("la_prod", [128, 2, T], BF16))
    eu = prod[0:88, 0, :]
    buf88 = e(nc.sbuf_tensor("la_b88", [44, T], F32))
    lnt = buf88[0:2, :]
    r2 = lnt
    rb44 = buf88[0:44, :]
    ae = prod[0:44, 1, :]
    abc_sb = e(nc.sbuf_tensor("la_abc", [128, 2, T], BF16))
    p1 = q_sb
    acc = e(nc.sbuf_tensor("la_acc", [128, T], BF16))
    g_sb = prod[:, 0, :]
    ghm = e(nc.sbuf_tensor("la_ghm", [128, KD, T], BF16))
    oo = acc
    ps_a = e(nc.psum_tensor("la_psa", [128, T], F32))
    ps_b = e(nc.psum_tensor("la_psb", [128, T], F32))
    with nc.sbuf_tensor("la_wstg", [128, 2, 448], F32) as wstg:
        wload_bf16(ph, e, dram("wqkv"), KD, 3 * D, "wqkv", stg=wstg, dst=wqkv)
        wload_bf16(ph, e, dram("wgate"), KD, D, "wg1", stg=wstg, dst=wgate)
        wload_bf16(ph, e, dram("wout1"), KD, D, "wo1", stg=wstg, dst=wout1)
        wload_bf16(ph, e, dram("Emat"), NOFF, 88, "em", stg=wstg, dst=em)
        smt2 = smt[:].rearrange("p a q -> p (a q)")
        for c0 in range(0, NOFF * 128, 448):
            cw = min(448, NOFF * 128 - c0)
            ph.load(wstg[0:44, 0, 0:cw], dram("Smat")[:, c0:c0 + cw],
                    wait=[("wcv", ph.c("wcv"))])
            ph.op("vector", lambda eng, c0=c0, cw=cw: eng.tensor_copy(
                smt2[:, c0:c0 + cw], wstg[0:44, 0, 0:cw]),
                inc=[("wcv", 1)], wait=[("ld", ph.c("ld"))])
        ph.load(wstg[0:88, 1, 0:44], dram("Gmat"), wait=[("wcv", ph.c("wcv"))])
        ph.op("vector", lambda eng: eng.tensor_copy(gm[:], wstg[0:88, 1, 0:44]),
              inc=[("wcv", 1)], wait=[("ld", ph.c("ld"))])
        ph.load(wstg[0:88, 0, 0:2], dram("Ehmat"), wait=[("wcv", ph.c("wcv"))])
        ph.op("vector", lambda eng: eng.tensor_copy(ehm[:], wstg[0:88, 0, 0:2]),
              inc=[("wcv", 1)], wait=[("ld", ph.c("ld"))])
        ph.load(ebc[:], dram("Ebc"))
        ph.load(pb[:], dram("pb88"))
        ph.load(bqkv[:], dram("bqkv2"))
        ph.load(bgate[:], dram("bgate2"))
        ph.load(bout[:], dram("bout2"))
        ph.close("wcv", "ld")

    ph.op("vector", lambda eng: eng.memset(kpad[:, 0:MAXOFF], 0.0))
    ph.op("vector", lambda eng: eng.memset(vpad[:, 0:MAXOFF], 0.0), inc=[("vec", 1)])
    ph.close("vec", "wcv", "ld")

    def mm8(wtile, mcol, rhs, dst_note, wait0, pst=None):
        if pst is None:
            pst = ps_a
        for c in range(NCH):
            for k in range(KD):
                ph.op("tensor", lambda eng, c=c, k=k, wtile=wtile, mcol=mcol, rhs=rhs, pst=pst: eng.matmul(
                    pst[:, c * CH:(c + 1) * CH], wtile[:, k, mcol * 128:(mcol + 1) * 128],
                    rhs[:, k, c * CH:(c + 1) * CH], start=(k == 0), stop=(k == KD - 1)),
                    inc=[("mm", 1)] if (c == NCH - 1 and k == KD - 1) else (),
                    wait=wait0 if (c == 0 and k == 0) else ())

    for b in range(NB):
        if b == 0:
            ph.load(xn1[:, :, :], xn1_i[:, 0:T].rearrange("(k p) t -> p k t", p=128))
            ph.close("ld")
        for hp in range(8):
            q_sp = ph.c("sp")
            mm8(wqkv, hp, xn1, "q",
                [("sp", ph.c("sp"))] + ([("ld", ph.c("ld"))] if hp == 0 else []))
            mm8(wqkv, 8 + hp, xn1, "k", [("sp", ph.c("sp"))], pst=ps_b)
            ph.op("scalar", lambda eng, hp=hp: eng.activation(
                q_sb[:], ps_a[:], AF.Identity, bias=bqkv[:, hp:hp + 1], scale=np.float32(1.0)),
                inc=[("sp", 1)], wait=[("mm", ph.c("mm") - 1), ("vec", ph.c("vec"))])
            mm8(wqkv, 16 + hp, xn1, "v", [("sp", ph.c("sp"))])
            ph.op("scalar", lambda eng, hp=hp: eng.activation(
                kpad[:, MAXOFF:], ps_b[:], AF.Identity, bias=bqkv[:, 8 + hp:9 + hp], scale=np.float32(1.0)),
                inc=[("sp", 1)], wait=[("mm", ph.c("mm") - 1)])
            ph.op("scalar", lambda eng, hp=hp: eng.activation(
                vpad[:, MAXOFF:], ps_a[:], AF.Identity, bias=bqkv[:, 16 + hp:17 + hp], scale=np.float32(1.0)),
                inc=[("sp", 1)], wait=[("mm", ph.c("mm"))])
            ph.close("sp")
            mm_base = ph.c("mm")
            for oi, off in enumerate(DOFFS):
                ph.op("vector", lambda eng, oi=oi, off=off: eng.tensor_tensor(
                    prod[:, oi % 2, :], q_sb[:], kpad[:, MAXOFF - off:MAXOFF - off + T], ALU.mult),
                    inc=[("vec", 1)],
                    wait=[("mm", mm_base + oi - 1)] if oi >= 2 else [("mm", mm_base)])
                for c in range(NCH):
                    ph.op("tensor", lambda eng, oi=oi, c=c: eng.matmul(
                        ps_b[0:88, c * CH:(c + 1) * CH], em[:, oi, :],
                        prod[:, oi % 2, c * CH:(c + 1) * CH], start=(oi == 0), stop=(oi == NOFF - 1)),
                        inc=[("mm", 1)] if c == NCH - 1 else (),
                        wait=[("vec", ph.c("vec"))] if c == 0 else ())
            ph.op("scalar", lambda eng, hp=hp: eng.activation(
                eu[:], ps_b[0:88, :], AF.Exp, bias=pb[:, hp:hp + 1], scale=np.float32(0.125)),
                inc=[("sp", 1)], wait=[("mm", ph.c("mm"))])
            ph.close("sp")

            def gate_chunk(c, hp=hp):
                for k in range(KD):
                    ph.op("tensor", lambda eng, c=c, k=k, hp=hp: eng.matmul(
                        ps_b[:, c * CH:(c + 1) * CH], wgate[:, k, hp * 128:(hp + 1) * 128],
                        xn1[:, k, c * CH:(c + 1) * CH], start=(k == 0), stop=(k == KD - 1)),
                        inc=[("mm", 1)] if k == KD - 1 else (),
                        wait=[("sp", ph.c("sp"))] if (c == 0 and k == 0) else ())

            for c in range(NCH):
                ph.op("tensor", lambda eng, c=c: eng.matmul(
                    ps_a[0:2, c * CH:(c + 1) * CH], ehm[:], eu[:, c * CH:(c + 1) * CH],
                    start=True, stop=True),
                    inc=[("mm", 1)] if c == NCH - 1 else ())
            gate_chunk(0)
            ph.op("scalar", lambda eng: eng.activation(lnt[:], ps_a[0:2, :], AF.Ln),
                  wait=[("mm", ph.c("mm") - 1)])
            ph.op("scalar", lambda eng: eng.activation(r2[:], lnt[:], AF.Exp, scale=np.float32(-1.0)),
                  inc=[("sp", 1)])
            gate_chunk(1)
            for c in range(NCH):
                ph.op("tensor", lambda eng, c=c: eng.matmul(
                    ps_a[0:44, c * CH:(c + 1) * CH], ebc[:], r2[:, c * CH:(c + 1) * CH],
                    start=True, stop=True),
                    inc=[("mm", 1)] if c == NCH - 1 else (),
                    wait=[("sp", ph.c("sp"))] if c == 0 else ())
            gate_chunk(2)
            ph.op("scalar", lambda eng: eng.copy(rb44[:], ps_a[0:44, :]),
                  inc=[("sp", 1)], wait=[("mm", ph.c("mm") - 1)])
            for c in range(NCH):
                ph.op("tensor", lambda eng, c=c: eng.matmul(
                    ps_a[64:108, c * CH:(c + 1) * CH], gm[:], eu[:, c * CH:(c + 1) * CH],
                    start=True, stop=True),
                    inc=[("mm", 1)] if c == NCH - 1 else ())
            gate_chunk(3)
            ph.op("vector", lambda eng: eng.tensor_tensor(ae[:], ps_a[64:108, :], rb44[:], ALU.mult),
                  inc=[("vec", 1)], wait=[("mm", ph.c("mm") - 1), ("sp", ph.c("sp"))])
            ph.op("scalar", lambda eng, hp=hp: eng.activation(
                g_sb[:], ps_b[:], AF.Sigmoid, bias=bgate[:, hp:hp + 1], scale=np.float32(1.0)),
                inc=[("gs", 1)], wait=[("mm", ph.c("mm")), ("vec", ph.c("vec"))])
            ph.close("vec")
            vb = ph.c("vec")
            for oi, off in enumerate(DOFFS):
                pp = ps_b if oi % 2 == 0 else ps_a
                for c in range(NCH):
                    ph.op("tensor", lambda eng, oi=oi, c=c, pp=pp: eng.matmul(
                        pp[:, c * CH:(c + 1) * CH], smt[:, oi, :],
                        ae[:, c * CH:(c + 1) * CH], start=True, stop=True),
                        inc=[("mm", 1)] if c == NCH - 1 else (),
                        wait=([("vec", vb), ("gs", ph.c("gs"))] if oi < 2
                              else [("sp", ph.c("sp") - 1)]) if c == 0 else ())
                ph.op("scalar", lambda eng, oi=oi, pp=pp: eng.copy(abc_sb[:, oi % 2, :], pp[:]),
                      inc=[("sp", 1)],
                      wait=[("mm", ph.c("mm"))] + ([("vec", ph.c("vec") - 1)] if oi >= 2 else []))
                ph.op("vector", lambda eng, oi=oi, off=off: eng.tensor_tensor(
                    p1[:], vpad[:, MAXOFF - off:MAXOFF - off + T], abc_sb[:, oi % 2, :], ALU.mult),
                    wait=[("sp", ph.c("sp"))])
                if oi == 0:
                    ph.op("vector", lambda eng: eng.tensor_copy(acc[:], p1[:]), inc=[("vec", 1)])
                else:
                    ph.op("vector", lambda eng: eng.tensor_tensor(acc[:], acc[:], p1[:], ALU.add),
                          inc=[("vec", 1)])
            ph.op("vector", lambda eng, hp=hp: eng.tensor_tensor(ghm[:, hp, :], acc[:], g_sb[:], ALU.mult),
                  inc=[("vec", 1)], wait=[("gs", ph.c("gs"))])
            ph.close("vec", "sp")
        if b + 1 < NB:
            ph.load(xn1[:, :, :], xn1_i[:, (b + 1) * T:(b + 2) * T].rearrange("(k p) t -> p k t", p=128),
                    wait=[("mm", ph.c("mm"))])
        for m in range(KD):
            pst = ps_a if m % 2 == 0 else ps_b
            mm8(wout1, m, ghm, "o",
                [("sp", ph.c("sp") - (1 if m >= 2 else 0)), ("st", ph.c("st"))], pst=pst)
            ph.op("scalar", lambda eng, m=m, pst=pst: eng.activation(
                oo[:], pst[:], AF.Identity, bias=bout[:, m:m + 1], scale=np.float32(1.0)),
                inc=[("sp", 1)], wait=[("mm", ph.c("mm")), ("st", ph.c("st"))])
            ph.store(at1_o[m * 128:(m + 1) * 128, b * T:(b + 1) * T], oo[:],
                     wait=[("sp", ph.c("sp"))])
        ph.close("sp", "st")


def build_l1ffn(ph, e, dram):
    nc = ph.nc
    xn1_i, x1_i, at1_i, h_s, out_o = dram("xn1"), dram("x1"), dram("at1"), dram("hmid"), dram("outx")
    with ExitStack() as we:
        ee = we.enter_context
        wg = wload_bf16_i32(ph, ee, dram("wgT"), KD, HID, "wg")
        wu = wload_bf16_i32(ph, ee, dram("wuT"), KD, HID, "wu")
        xn1 = ee(nc.sbuf_tensor("lf_xn1", [128, KD, T], BF16))
        sg = ee(nc.sbuf_tensor("lf_sg", [128, T], BF16))
        hh = ee(nc.sbuf_tensor("lf_h", [128, T], BF16))
        ps_g = ee(nc.psum_tensor("lf_psg", [128, 2, T // 2], F32))
        ps_u = ee(nc.psum_tensor("lf_psu", [128, 2, T // 2], F32))
        ph.close("wcv", "ld")
        for b in range(NB):
            ph.load(xn1[:, :, :], xn1_i[:, b * T:(b + 1) * T].rearrange("(k p) t -> p k t", p=128))
            ph.close("ld")
            for hm in range(KH):
                for half in range(2):
                    HB = T // 2
                    for c in range(2):
                        for k in range(KD):
                            ph.op("tensor", lambda eng, hm=hm, half=half, c=c, k=k: eng.matmul(
                                ps_g[:, half, c * CH:(c + 1) * CH], wg[:, k, hm * 128:(hm + 1) * 128],
                                xn1[:, k, half * HB + c * CH:half * HB + (c + 1) * CH],
                                start=(k == 0), stop=(k == KD - 1)),
                                wait=[("sp", ph.c("sp") - 1)] if (c == 0 and k == 0) else ())
                    for c in range(2):
                        for k in range(KD):
                            ph.op("tensor", lambda eng, hm=hm, half=half, c=c, k=k: eng.matmul(
                                ps_u[:, half, c * CH:(c + 1) * CH], wu[:, k, hm * 128:(hm + 1) * 128],
                                xn1[:, k, half * HB + c * CH:half * HB + (c + 1) * CH],
                                start=(k == 0), stop=(k == KD - 1)),
                                inc=[("mm", 1)] if (c == 1 and k == KD - 1) else (),
                                wait=[("vec", ph.c("vec") - 1)] if (c == 0 and k == 0) else ())
                    ph.op("scalar", lambda eng, half=half: eng.activation(
                        sg[:, half * HB:(half + 1) * HB], ps_g[:, half, :], AF.Silu,
                        scale=np.float32(1.0 / 127.0)),
                        inc=[("sp", 1)], wait=[("mm", ph.c("mm"))])
                    ph.op("vector", lambda eng, half=half: eng.tensor_tensor(
                        hh[:, half * HB:(half + 1) * HB], ps_u[:, half, :], sg[:, half * HB:(half + 1) * HB],
                        ALU.mult),
                        inc=[("vec", 1)], wait=[("sp", ph.c("sp")), ("st", ph.c("st"))])
                ph.store(h_s[hm * 128:(hm + 1) * 128, b * T:(b + 1) * T], hh[:],
                         wait=[("vec", ph.c("vec"))])
            ph.close("st", "vec", "sp")
    wd = wload_bf16_i32(ph, e, dram("wdT"), KH, D, "wd")
    hb = e(nc.sbuf_tensor("lf_hb", [128, KH, T], BF16))
    xs = e(nc.sbuf_tensor("lf_xs", [128, 2, T], F32))
    as_ = e(nc.sbuf_tensor("lf_as", [128, 2, T], BF16))
    d_sb = e(nc.sbuf_tensor("lf_d", [128, T], F32))
    t_sb = e(nc.sbuf_tensor("lf_t", [128, T], F32))
    o_sb = d_sb
    ps_d = e(nc.psum_tensor("lf_psd", [128, 2, T], F32))
    ph.close("wcv", "ld", "st")
    for b in range(NB):
        ph.load(hb[:, :, :], h_s[:, b * T:(b + 1) * T].rearrange("(k p) t -> p k t", p=128))
        ph.close("ld")
        for m in range(KD):
            ph.load(xs[:, m % 2, :], x1_i[m * 128:(m + 1) * 128, b * T:(b + 1) * T],
                    wait=[("vec", ph.c("vec"))])
            ph.load(as_[:, m % 2, :], at1_i[m * 128:(m + 1) * 128, b * T:(b + 1) * T])
            for c in range(NCH):
                for k in range(KH):
                    ph.op("tensor", lambda eng, m=m, c=c, k=k: eng.matmul(
                        ps_d[:, m % 2, c * CH:(c + 1) * CH], wd[:, k, m * 128:(m + 1) * 128],
                        hb[:, k, c * CH:(c + 1) * CH], start=(k == 0), stop=(k == KH - 1)),
                        inc=[("mm", 1)] if (c == NCH - 1 and k == KH - 1) else (),
                        wait=[("sp", ph.c("sp") - 1)] if (c == 0 and k == 0) else ())
            ph.op("scalar", lambda eng, m=m: eng.activation(d_sb[:], ps_d[:, m % 2, :], AF.Identity, scale=np.float32(QS2)),
                  inc=[("sp", 1)], wait=[("mm", ph.c("mm")), ("vec", ph.c("vec")), ("st", ph.c("st"))])
            ph.op("vector", lambda eng, m=m: eng.tensor_tensor(t_sb[:], d_sb[:], xs[:, m % 2, :], ALU.add),
                  wait=[("sp", ph.c("sp")), ("ld", ph.c("ld")), ("st", ph.c("st"))])
            ph.op("vector", lambda eng, m=m: eng.tensor_tensor(o_sb[:], t_sb[:], as_[:, m % 2, :], ALU.add),
                  inc=[("vec", 2)])
            ph.store(out_o[m * 128:(m + 1) * 128, b * T:(b + 1) * T], o_sb[:],
                     wait=[("vec", ph.c("vec"))])
        ph.close("vec", "st")


PHASES = [
    ("ln0", lambda ph, e, dram: build_ln(ph, e, dram, "x", "xn"),
     ["x"], ["xn"]),
    ("l0ffn", build_l0ffn, ["xn", "wrT", "wkT", "wvT", "mur", "muk"], ["ffn0"]),
    ("mambapre", build_mamba_pre,
     ["xn", "w_in", "dwdiag", "pw", "cb2", "wx"], ["sz", "xc", "dlr", "bc"]),
    ("scan", build_scan,
     ["xc", "dlr", "bc", "sz", "wdt", "bdt2", "alog", "dvec", "esel"], ["ywz"]),
    ("l0out", build_l0out, ["ywz", "x", "ffn0", "wout"], ["x1"]),
    ("ln1", lambda ph, e, dram: build_ln(ph, e, dram, "x1", "xn1"),
     ["x1"], ["xn1"]),
    ("l1attn", build_l1attn,
     ["xn1", "wqkv", "bqkv2", "wgate", "bgate2", "wout1", "bout2", "pb88",
      "Emat", "Gmat", "Smat", "Ehmat", "Ebc"], ["at1"]),
    ("l1ffn", build_l1ffn, ["xn1", "x1", "at1", "wgT", "wuT", "wdT"], ["outx"]),
]


def build_phase_prog(name, builder, ins, outs):
    nc = bass.Bass("TRN2", target_bir_lowering=False)
    ctx = ExitStack()
    aps = {}
    for n in ins:
        shape, dt = WSPEC[n] if n in WSPEC else SCRATCH[n]
        aps[n] = nc.dram_tensor(n, list(shape), dt, kind="ExternalInput").ap()
    for n in outs:
        shape, dt = SCRATCH[n]
        aps[n] = nc.dram_tensor(n, list(shape), dt, kind="ExternalOutput").ap()
    if name == "l1ffn":
        shape, dt = SCRATCH["hmid"]
        aps["hmid"] = nc.dram_tensor("hmid", list(shape), dt).ap()
    with ctx:
        ph = Ph(nc, ctx, name)
        builder(ph, ctx.enter_context, lambda n: aps[n])
        ph.finalize()
        ph.emit()
    return nc


class NCProxy:

    def __init__(self, nc, prefix):
        self._nc = nc
        self._prefix = prefix

    def __getattr__(self, a):
        return getattr(self._nc, a)

    def sbuf_tensor(self, name, *a, **k):
        return self._nc.sbuf_tensor(self._prefix + name, *a, **k)

    def psum_tensor(self, name, *a, **k):
        return self._nc.psum_tensor(self._prefix + name, *a, **k)


def build_fused():
    nc = bass.Bass("TRN2", target_bir_lowering=False)
    aps = {}
    for n, (shape, dt) in WSPEC.items():
        aps[n] = nc.dram_tensor(n, list(shape), dt, kind="ExternalInput").ap()
    for n, (shape, dt) in SCRATCH.items():
        if n == "outx":
            aps[n] = nc.dram_tensor(n, list(shape), dt, kind="ExternalOutput").ap()
        else:
            aps[n] = nc.dram_tensor(n, list(shape), dt).ap()
    ctx = ExitStack()
    with ctx:
        ph = Ph(nc, ctx, "mk")
        for pi, (name, builder, ins, outs) in enumerate(PHASES):
            ph.nc = NCProxy(nc, f"p{pi}_")
            with ExitStack() as pctx:
                builder(ph, pctx.enter_context, lambda n: aps[n])
                ph.close(*list(ph.cnt.keys()))
        ph.nc = nc
        ph.finalize()
        ph.emit()
    return nc


_PROG_CACHE = {}


def kernel(**inputs):
    w = host_prep(inputs)
    x = inputs["L0_attn_x"]
    state = []
    for c in range(NCORES):
        xs = np.ascontiguousarray(
            x[c * NB:(c + 1) * NB].transpose(2, 0, 1).reshape(D, BT))
        state.append({"x": xs, **w})

    if MODE == "fused":
        if "fused" not in _PROG_CACHE:
            _PROG_CACHE["fused"] = build_fused()
        nc = _PROG_CACHE["fused"]
        in_maps = [dict(state[c]) for c in range(NCORES)]
        res = run_bass_kernel_spmd(nc, in_maps, core_ids=list(range(NCORES)))
        for c in range(NCORES):
            state[c]["outx"] = np.asarray(res.results[c]["outx"])
    else:
        for name, builder, ins, outs in PHASES:
            key = name
            if key not in _PROG_CACHE:
                _PROG_CACHE[key] = build_phase_prog(name, builder, ins, outs)
            nc = _PROG_CACHE[key]
            in_maps = [{n: state[c][n] for n in ins} for c in range(NCORES)]
            res = run_bass_kernel_spmd(nc, in_maps, core_ids=list(range(NCORES)))
            for c in range(NCORES):
                for n in outs:
                    state[c][n] = np.asarray(res.results[c][n])

    out = np.empty((32, T, D), np.float32)
    for c in range(NCORES):
        o = state[c]["outx"].reshape(D, NB, T).transpose(1, 2, 0)
        out[c * NB:(c + 1) * NB] = o
    return out
